# version 05 (final)
# baseline (speedup 1.0000x reference)
import numpy as np
import ml_dtypes
import concourse.bass as bass
import concourse.mybir as mybir
from concourse.bass_utils import run_bass_kernel_spmd

F32 = mybir.dt.float32
BF16 = mybir.dt.bfloat16
I32 = mybir.dt.int32
AF = mybir.ActivationFunctionType
ALU = mybir.AluOpType
AX = mybir.AxisListType

D = 2048
T = 4096
TH = T // 2
NH = 16
HD = 128
BLK = 256
CHK = 128
HF0 = 8192
NE = 8
HE = 1408
HS = 2816
EPS0 = 1e-6
EPSL = 1e-5
FEPS = 1.1920929e-07
THRC = 0.5 + 2.0 ** -11
NC16 = D // 128

NCORES = 8
BF16NP = ml_dtypes.bfloat16



class Res:
    __slots__ = ("name", "w", "rs")

    def __init__(self, name=""):
        self.name = name
        self.w = None
        self.rs = []


class _Op:
    __slots__ = ("fn", "deps", "kind", "label")

    def __init__(self, fn, deps, kind, label):
        self.fn = fn
        self.deps = deps
        self.kind = kind
        self.label = label


class Scope:

    def __init__(self, nc):
        self.nc = nc
        self.guards = []

    def sb(self, name, shape, dtype):
        g = self.nc.sbuf_tensor(name, shape, dtype)
        h = g.__enter__()
        self.guards.append(g)
        return h

    def ps(self, name, shape, dtype=F32):
        g = self.nc.psum_tensor(name, shape, dtype)
        h = g.__enter__()
        self.guards.append(g)
        return h

    def close(self):
        for g in reversed(self.guards):
            g.__exit__(None, None, None)
        self.guards = []


class Prog:
    QUEUES = ("sync", "scalar", "vector", "tensor", "gpsimd")
    NSD = 6

    def __init__(self, nc):
        self.nc = nc
        self.ops = {q: [] for q in self.QUEUES}
        self.sems = {}
        self._barrier = []
        for q in self.QUEUES:
            self.sems[(q, "i")] = nc.alloc_semaphore(name=f"s_{q}_i")
            for s in range(self.NSD):
                self.sems[(q, "d", s)] = nc.alloc_semaphore(name=f"s_{q}_d{s}")

    def barrier(self):
        bar = []
        for q in self.QUEUES:
            if not self.ops[q]:
                continue
            bar.append((q, len(self.ops[q]) - 1))
            ndma = 0
            for i in range(len(self.ops[q]) - 1, -1, -1):
                if self.ops[q][i].kind == "d":
                    bar.append((q, i))
                    ndma += 1
                    if ndma >= self.NSD:
                        break
        self._barrier = bar

    def _add(self, queue, fn, reads, writes, deps, kind, label):
        myid = (queue, len(self.ops[queue]))
        dep = list(deps or [])
        dep.extend(self._barrier)
        for r in reads or []:
            if r.w is not None:
                dep.append(r.w)
        for w in writes or []:
            if w.w is not None:
                dep.append(w.w)
            dep.extend(w.rs)
        for r in reads or []:
            r.rs.append(myid)
        for w in writes or []:
            w.w = myid
            w.rs = []
        self.ops[queue].append(_Op(fn, dep, kind, label))
        return myid

    def op(self, queue, fn, reads=None, writes=None, deps=None, label=""):
        return self._add(queue, fn, reads, writes, deps, "i", label)

    def dma(self, queue, out, in_, reads=None, writes=None, deps=None, label="", **kw):
        def fn(eng):
            return eng.dma_start(out=out, in_=in_, **kw)
        return self._add(queue, fn, reads, writes, deps, "d", label)

    def raw(self, queue, fn, kind="d", reads=None, writes=None, deps=None, label=""):
        return self._add(queue, fn, reads, writes, deps, kind, label)

    def finalize(self):
        nc = self.nc
        observed = set()
        for q in self.QUEUES:
            for o in self.ops[q]:
                observed.update(o.deps)
        semval = {}
        counts = {}
        throttle = {}
        ndma = {q: 0 for q in self.QUEUES}
        for q in self.QUEUES:
            for i, o in enumerate(self.ops[q]):
                if o.kind == "d":
                    s = ndma[q] % self.NSD
                    ndma[q] += 1
                    k = (q, "d", s)
                    counts[k] = counts.get(k, 0) + 16
                    semval[(q, i)] = (k, counts[k])
                    if counts[k] > 16:
                        throttle[(q, i)] = (k, counts[k] - 16)
                elif (q, i) in observed:
                    k = (q, "i")
                    counts[k] = counts.get(k, 0) + 1
                    semval[(q, i)] = (k, counts[k])
        waits = {q: [None] * len(self.ops[q]) for q in self.QUEUES}
        for q in self.QUEUES:
            hw = {}
            for i, o in enumerate(self.ops[q]):
                ws = []
                for dq, di in o.deps:
                    if dq == q and self.ops[dq][di].kind == "i" and q == "tensor":
                        continue
                    k, v = semval[(dq, di)]
                    if hw.get(k, 0) < v:
                        hw[k] = v
                        ws.append((k, v))
                waits[q][i] = ws
        ptr = {q: 0 for q in self.QUEUES}
        done = set()
        total = sum(len(v) for v in self.ops.values())
        progressed = True
        n_done = 0
        while n_done < total:
            progressed = False
            for q in self.QUEUES:
                while ptr[q] < len(self.ops[q]):
                    o = self.ops[q][ptr[q]]
                    if all((dq == q and di < ptr[q]) or (dq, di) in done
                           for dq, di in o.deps):
                        done.add((q, ptr[q]))
                        ptr[q] += 1
                        n_done += 1
                        progressed = True
                    else:
                        break
            if not progressed:
                stall = {q: (ptr[q], len(self.ops[q]),
                             self.ops[q][ptr[q]].label if ptr[q] < len(self.ops[q]) else "-",
                             [d for d in self.ops[q][ptr[q]].deps
                              if not ((d[0] == q and d[1] < ptr[q]) or d in done)][:4]
                             if ptr[q] < len(self.ops[q]) else [])
                         for q in self.QUEUES}
                raise RuntimeError(f"deadlock in program build: {stall}")
        sems = self.sems
        ops = self.ops
        wt = waits
        sv = semval
        with nc.Block() as block:
            def emit(q):
                def _(eng):
                    for i, o in enumerate(ops[q]):
                        th = throttle.get((q, i))
                        if th is not None:
                            eng.wait_ge(sems[th[0]], th[1])
                        for k, v in wt[q][i]:
                            eng.wait_ge(sems[k], v)
                        try:
                            inst = o.fn(eng)
                        except Exception as ex:
                            raise RuntimeError(
                                f"emit failed at {q}[{i}] label={o.label!r}: {ex}"
                            ) from ex
                        if (q, i) in sv:
                            k, _v = sv[(q, i)]
                            inst.then_inc(sems[k], 16 if k[1] == "d" else 1)
                    for s in range(self.NSD):
                        tot = counts.get((q, "d", s), 0)
                        if tot:
                            eng.wait_ge(sems[(q, "d", s)], tot)
                return _
            block.sync(emit("sync"))
            block.scalar(emit("scalar"))
            block.vector(emit("vector"))
            block.tensor(emit("tensor"))
            block.gpsimd(emit("gpsimd"))
        return nc



def _prep_host(inputs):
    f32 = np.float32

    def T_(a):
        return np.ascontiguousarray(np.asarray(a, dtype=f32).T)

    shared = {}
    shared["wqkvuT"] = T_(inputs["L0_attn_W_qkvu"])
    shared["woutT"] = T_(inputs["L0_attn_W_out"])
    shared["wguT"] = T_(inputs["L0_ffn_w_gate_up"])
    shared["wprojT"] = T_(inputs["L0_ffn_w_proj"])
    shared["wqT"] = T_(inputs["L1_attn_wq"])
    shared["wkT"] = T_(inputs["L1_attn_wk"])
    shared["wvT"] = T_(inputs["L1_attn_wv"])
    shared["woT"] = T_(inputs["L1_attn_wo"])
    shared["gatewT"] = T_(inputs["L1_ffn_gate_w"])
    shared["wgate_e"] = np.ascontiguousarray(np.asarray(inputs["L1_ffn_w_gate"], f32))
    shared["wup_e"] = np.ascontiguousarray(np.asarray(inputs["L1_ffn_w_up"], f32))
    shared["wdown_e"] = np.ascontiguousarray(np.asarray(inputs["L1_ffn_w_down"], f32))
    shared["shgate_col"] = np.ascontiguousarray(
        np.asarray(inputs["L1_ffn_shgate_w"], f32).reshape(D, 1))
    shared["shwgate"] = np.ascontiguousarray(np.asarray(inputs["L1_ffn_shw_gate"], f32))
    shared["shwup"] = np.ascontiguousarray(np.asarray(inputs["L1_ffn_shw_up"], f32))
    shared["shwdown"] = np.ascontiguousarray(np.asarray(inputs["L1_ffn_shw_down"], f32))
    mem = np.asarray(inputs["L1_attn_memories"], f32)
    mn = np.asarray(inputs["L1_attn_memory_norms"], f32)
    shared["memcat"] = np.ascontiguousarray(np.concatenate([mem, mn[..., None]], axis=-1))
    lm = np.asarray(inputs["L1_attn_landmarks"], f32)
    shared["lmT"] = np.ascontiguousarray(lm.transpose(0, 2, 1))
    shared["kcT"] = np.ascontiguousarray(np.asarray(inputs["L1_attn_key_counts"], f32).T)
    shared["gate_row"] = np.ascontiguousarray(
        np.asarray(inputs["L1_attn_gate"], f32).reshape(1, NH))
    shared["slope_row"] = np.ascontiguousarray(
        np.asarray(inputs["L0_attn_slope_rate"], f32).reshape(1, NH))

    s_ = np.arange(BLK, dtype=f32)
    m_ = np.arange(BLK, dtype=f32)
    idxm = m_[None, :] - s_[:, None]
    shared["idxT"] = np.ascontiguousarray(np.where(idxm >= 0, idxm, 1e6).astype(f32))
    shared["trilT"] = np.ascontiguousarray((idxm >= 0).astype(f32))
    m1c = np.arange(CHK, dtype=f32)
    shared["maskT1"] = np.ascontiguousarray(
        (m1c[None, :] - m1c[:, None] >= 0).astype(f32))
    shared["arrq"] = np.ascontiguousarray(
        np.broadcast_to(np.arange(1, BLK + 1, dtype=f32)[None, :], (128, BLK)).copy())
    shared["arrk"] = np.ascontiguousarray(
        np.broadcast_to((BLK - np.arange(1, BLK + 1, dtype=f32))[None, :], (128, BLK)).copy())
    shared["ones_col_bf"] = np.ones((128, 1), dtype=BF16NP)
    shared["ones_col_f"] = np.ones((128, 1), dtype=f32)
    shared["ones_row_f"] = np.ones((1, 128), dtype=f32)
    shared["ident_bf"] = np.eye(128, dtype=BF16NP)
    shared["ident_f"] = np.eye(128, dtype=f32)
    gs = np.zeros((128, 128), dtype=BF16NP)
    gs[:64, 0] = 1
    gs[64:, 64] = 1
    shared["gsel_bf"] = gs
    es = np.zeros((8, 8 * 128), dtype=BF16NP)
    for e in range(8):
        es[e, e * 128:(e + 1) * 128] = 1
    shared["esel_bf"] = es

    x = np.asarray(inputs["L0_attn_x"], f32)
    per_core = []
    for c in range(NCORES):
        b, half = c // 2, c & 1
        own = half * TH
        offv = ([own + 128 * i for i in range(16)] +
                [own + 512 * t for t in range(4)] + [own, own + 1024] + [0] * 10 +
                [own // 128 + i for i in range(16)] + [0] * 16)
        m = {"xT": np.ascontiguousarray(x[b].T),
             "offv": np.array([offv], np.int32),
             "flag": np.array([[float(half)]], np.float32)}
        per_core.append(m)
    return shared, per_core



def emit_p1(p, d, sc):
    nc = p.nc
    NT = T // 512
    NTH = NT // 2

    sx = [sc.sb("p1_sx0", [128, NC16 * 512], F32)]
    sx.append(sx[0])
    sxbf = sc.sb("p1_sxbf", [128, NC16 * 512], BF16)
    ssq2 = [sc.sb(f"p1_ssq{i}", [128, 512], BF16) for i in range(2)]
    onesb = sc.sb("p1_onesb", [128, 1], BF16)
    onesf = sc.sb("p1_onesf", [1, 128], F32)
    rows = sc.sb("p1_rows", [1, 6 * 512], F32)
    mrep = sc.sb("p1_mrep", [128, 512], F32)
    rrep = sc.sb("p1_rrep", [128, 512], F32)
    tmpf = sc.sb("p1_tmpf", [128, 512], F32)
    sxn = sc.sb("p1_sxn", [128, NC16 * TH], BF16)
    swg = [sc.sb(f"p1_swg{i}", [128, NC16 * 512], BF16) for i in range(2)]
    sev = [sc.sb(f"p1_sev{i}", [128, 512], BF16) for i in range(4)]

    ps_mm = [sc.ps(f"p1_psmm{i}", [128, 512], F32) for i in range(2)]
    ps_s1 = sc.ps("p1_pss1", [1, 512], F32)
    ps_s2 = sc.ps("p1_pss2", [1, 512], F32)
    ps_rep = sc.ps("p1_psrep", [128, 512], F32)

    r_sx = [Res("sx0")]
    r_sx.append(r_sx[0])
    r_sxbf = Res("sxbf")
    r_ssq2 = [Res(f"ssq{i}") for i in range(2)]
    r_ones = Res("ones")
    r_rows = Res("rows")
    r_reps = Res("reps")
    r_tmpf = Res("tmpf")
    r_xn = Res("xn")
    r_wg = [Res(f"wg{i}") for i in range(2)]
    r_ev = [Res(f"ev{i}") for i in range(4)]
    r_psmm = [Res(f"psmm{i}") for i in range(2)]
    r_ps1 = Res("ps1")
    r_ps2 = Res("ps2")
    r_psrep = Res("psrep")

    xr = d["xT"].ap().rearrange("(c p) t -> p c t", p=128)
    wr = d["wqkvuT"].ap().rearrange("(c p) j -> p c j", p=128)
    xnr = d["xn0"].ap().rearrange("(c p) t -> p c t", p=128)

    p.dma("sync", onesf[:], d["ones_row_f"][:, :], writes=[r_ones], label="onesf")
    p.dma("gpsimd", onesb[:], d["ones_col_bf"][:, :], writes=[r_ones], label="onesb")

    sx3 = [b[:].rearrange("p (c t) -> p c t", c=NC16) for b in sx]
    sxb3 = sxbf[:].rearrange("p (c t) -> p c t", c=NC16)
    sxn3 = sxn[:].rearrange("p (c t) -> p c t", c=NC16)
    swg3 = [b[:].rearrange("p (c j) -> p c j", c=NC16) for b in swg]

    for h in range(2):
        for tc in range(NTH):
            tq = h * NTH + tc
            bi = tq % 2
            for piece in range(4):
                p.dma("sync", sx3[bi][:, piece * 4:(piece + 1) * 4, :],
                      xr[:, piece * 4:(piece + 1) * 4, tq * 512:(tq + 1) * 512],
                      writes=[r_sx[bi]], label=f"ldx{tq}")
                p.dma("gpsimd", sxb3[:, piece * 4:(piece + 1) * 4, :],
                      xr[:, piece * 4:(piece + 1) * 4, tq * 512:(tq + 1) * 512],
                      writes=[r_sxbf], label=f"ldxb{tq}")
            for c in range(NC16):
                p.op("tensor", lambda e, c=c: e.matmul(
                    ps_s1[:], onesb[:, 0:1], sxb3[:, c, :],
                    start=(c == 0), stop=(c == NC16 - 1)),
                    reads=[r_ones, r_sxbf], writes=[r_ps1], label=f"mms1_{tq}")
            for c in range(NC16):
                sq = c % 2
                p.op("scalar", lambda e, bi=bi, c=c, sq=sq: e.activation(
                    ssq2[sq][:], sx3[bi][:, c, :], AF.Square),
                    reads=[r_sx[bi]], writes=[r_ssq2[sq]], label=f"sq{tq}.{c}")
                p.op("tensor", lambda e, c=c, sq=sq: e.matmul(
                    ps_s2[:], onesb[:, 0:1], ssq2[sq][:],
                    start=(c == 0), stop=(c == NC16 - 1)),
                    reads=[r_ones, r_ssq2[sq]], writes=[r_ps2], label=f"mms2_{tq}.{c}")
            R = rows
            p.op("vector", lambda e: e.tensor_scalar(R[:, 0:512], ps_s1[0:1, :], 1.0 / D, None, ALU.mult),
                 reads=[r_ps1], writes=[r_rows], label=f"mrow{tq}")
            p.op("vector", lambda e: e.tensor_scalar(R[:, 512:1024], ps_s2[0:1, :], 1.0 / D, None, ALU.mult),
                 reads=[r_ps2], writes=[r_rows], label=f"msq{tq}")
            p.op("vector", lambda e: e.tensor_tensor(R[:, 1024:1536], R[:, 0:512], R[:, 0:512], ALU.mult),
                 reads=[r_rows], writes=[r_rows])
            p.op("vector", lambda e: e.tensor_tensor(R[:, 1024:1536], R[:, 512:1024], R[:, 1024:1536], ALU.subtract),
                 reads=[r_rows], writes=[r_rows])
            p.op("vector", lambda e: e.tensor_scalar(R[:, 1024:1536], R[:, 1024:1536], EPSL, None, ALU.add),
                 reads=[r_rows], writes=[r_rows])
            p.op("scalar", lambda e: e.activation(R[:, 1536:2048], R[:, 1024:1536], AF.Sqrt),
                 reads=[r_rows], writes=[r_rows], label=f"sqrt{tq}")
            p.op("vector", lambda e: e.reciprocal(R[:, 2048:2560], R[:, 1536:2048]),
                 reads=[r_rows], writes=[r_rows])
            p.op("vector", lambda e: e.tensor_tensor(R[:, 2560:3072], R[:, 0:512], R[:, 2048:2560], ALU.mult),
                 reads=[r_rows], writes=[r_rows])
            p.op("tensor", lambda e: e.matmul(ps_rep[:], onesf[:, :], R[0:1, 2048:2560],
                                              start=True, stop=True),
                 reads=[r_ones, r_rows], writes=[r_psrep], label=f"reprstd{tq}")
            p.op("vector", lambda e: e.tensor_copy(rrep[:], ps_rep[:]),
                 reads=[r_psrep], writes=[r_reps])
            p.op("tensor", lambda e: e.matmul(ps_rep[:], onesf[:, :], R[0:1, 2560:3072],
                                              start=True, stop=True),
                 reads=[r_ones, r_rows], writes=[r_psrep], label=f"repmr{tq}")
            p.op("vector", lambda e: e.tensor_copy(mrep[:], ps_rep[:]),
                 reads=[r_psrep], writes=[r_reps])
            for c in range(NC16):
                p.op("vector", lambda e, bi=bi, c=c: e.tensor_tensor(
                    tmpf[:], sx3[bi][:, c, :], rrep[:], ALU.mult),
                    reads=[r_sx[bi], r_reps], writes=[r_tmpf])
                p.op("vector", lambda e, c=c, tc=tc: e.tensor_tensor(
                    sxn3[:, c, tc * 512:(tc + 1) * 512], tmpf[:], mrep[:], ALU.subtract),
                    reads=[r_tmpf, r_reps], writes=[r_xn], label=f"xn{tq}.{c}")
            p.dma("sync", xnr[:, :, tq * 512:(tq + 1) * 512],
                  sxn3[:, :, tc * 512:(tc + 1) * 512],
                  reads=[r_xn], label=f"stxn{tq}")

        for g in range(16):
            k = h * 16 + g
            wb = k % 2
            for piece in range(4):
                p.dma("gpsimd", swg3[wb][:, piece * 4:(piece + 1) * 4, :],
                      wr[:, piece * 4:(piece + 1) * 4, g * 512:(g + 1) * 512],
                      writes=[r_wg[wb]], label=f"ldw{k}")
            fm = g < 8
            out_d = (d["sq0"] if g < 4 else d["sk0"]) if fm else (d["v0"] if g < 12 else d["u0"])
            gj = g % 4
            for i in range(16):
                pi = i % 2
                ei = i % 4
                if fm:
                    jj, t = i // NTH, i % NTH
                    for c in range(NC16):
                        p.op("tensor", lambda e, wb=wb, c=c, jj=jj, t=t, pi=pi: e.matmul(
                            ps_mm[pi][:],
                            swg3[wb][:, c, jj * 128:(jj + 1) * 128],
                            sxn3[:, c, t * 512:(t + 1) * 512],
                            start=(c == 0), stop=(c == NC16 - 1)),
                            reads=[r_wg[wb], r_xn], writes=[r_psmm[pi]],
                            label=f"qkmm{k}.{i}")
                    p.op("scalar", lambda e, pi=pi, ei=ei: e.activation(
                        sev[ei][:], ps_mm[pi][:], AF.Silu),
                        reads=[r_psmm[pi]], writes=[r_ev[ei]], label=f"silu{k}.{i}")
                    dst = out_d.ap()[gj * 512 + jj * 128:gj * 512 + (jj + 1) * 128,
                                     h * TH + t * 512:h * TH + (t + 1) * 512]
                else:
                    ts = i
                    for c in range(NC16):
                        p.op("tensor", lambda e, wb=wb, c=c, ts=ts, pi=pi: e.matmul(
                            ps_mm[pi][:],
                            sxn3[:, c, ts * 128:(ts + 1) * 128],
                            swg3[wb][:, c, :],
                            start=(c == 0), stop=(c == NC16 - 1)),
                            reads=[r_wg[wb], r_xn], writes=[r_psmm[pi]],
                            label=f"vumm{k}.{i}")
                    p.op("vector", lambda e, pi=pi, ei=ei: e.tensor_copy(
                        sev[ei][:], ps_mm[pi][:]),
                        reads=[r_psmm[pi]], writes=[r_ev[ei]], label=f"vuev{k}.{i}")
                    dst = out_d.ap()[h * TH + ts * 128:h * TH + (ts + 1) * 128,
                                     gj * 512:(gj + 1) * 512]
                p.dma("sync", dst, sev[ei][:], reads=[r_ev[ei]], label=f"st{k}.{i}")


def build_p1(shared_shapes, kind_inter="ExternalOutput"):
    nc = bass.Bass("TRN2", target_bir_lowering=False)
    d = {}
    d["xT"] = nc.dram_tensor("xT", [D, T], F32, kind="ExternalInput")
    d["wqkvuT"] = nc.dram_tensor("wqkvuT", [D, 4 * D], F32, kind="ExternalInput")
    d["ones_col_bf"] = nc.dram_tensor("ones_col_bf", [128, 1], BF16, kind="ExternalInput")
    d["ones_row_f"] = nc.dram_tensor("ones_row_f", [1, 128], F32, kind="ExternalInput")
    d["xn0"] = nc.dram_tensor("xn0", [D, T], BF16, kind=kind_inter)
    d["sq0"] = nc.dram_tensor("sq0", [D, T], BF16, kind=kind_inter)
    d["sk0"] = nc.dram_tensor("sk0", [D, T], BF16, kind=kind_inter)
    d["v0"] = nc.dram_tensor("v0", [T, D], BF16, kind=kind_inter)
    d["u0"] = nc.dram_tensor("u0", [T, D], BF16, kind=kind_inter)
    p = Prog(nc)
    sc = Scope(nc)
    emit_p1(p, d, sc)
    sc.close()
    p.finalize()
    return nc



def emit_p2(p, d, sc):
    nc = p.nc
    NB = T // BLK

    slope = sc.sb("p2_slope", [1, NH], F32)
    onesf = sc.sb("p2_onesf", [1, 128], F32)
    negs = sc.sb("p2_negs", [128, NH], F32)
    bdrow = sc.sb("p2_bdrow", [1, NH], F32)
    bdrep = sc.sb("p2_bdrep", [128, NH], F32)
    idxt = sc.sb("p2_idxt", [128, 2 * 256], F32)
    trilt = sc.sb("p2_trilt", [128, 2 * 256], F32)
    arrq = sc.sb("p2_arrq", [128, 256], F32)
    arrk = sc.sb("p2_arrk", [128, 256], F32)
    identb = sc.sb("p2_identb", [128, 128], BF16)
    etmp = sc.sb("p2_etmp", [128, 256], F32)
    ddt = sc.sb("p2_ddt", [128, NH * 2 * 256], F32)
    qdec = sc.sb("p2_qdec", [128, NH * 256], BF16)
    kdec = sc.sb("p2_kdec", [128, NH * 256], BF16)
    kv = sc.sb("p2_kv", [128, NH * 128], F32)
    kvbf = sc.sb("p2_kvbf", [128, 128], BF16)
    sqt = [sc.sb(f"p2_sqt{i}", [128, NH * 256], BF16) for i in range(2)]
    skt = [sc.sb(f"p2_skt{i}", [128, NH * 256], BF16) for i in range(2)]
    vt = [sc.sb(f"p2_vt{i}", [128, 2 * D], BF16) for i in range(2)]
    qd = sc.sb("p2_qd", [128, 256], BF16)
    kd = sc.sb("p2_kd", [128, 256], BF16)
    kdt = sc.sb("p2_kdt", [128, 2 * 128], BF16)
    sT = [sc.sb(f"p2_sT{i}", [128, 256], BF16) for i in range(2)]
    oblk = [sc.sb(f"p2_oblk{i}", [128, 2 * D], BF16) for i in range(2)]

    ps_sc = [sc.ps(f"p2_pssc{i}", [128, 256], F32) for i in range(2)]
    ps_o = [sc.ps(f"p2_pso{i}", [128, 128], F32) for i in range(2)]
    ps_tr = sc.ps("p2_pstr", [128, 128], BF16)
    ps_kv = sc.ps("p2_pskv", [128, 128], F32)
    ps_rep = sc.ps("p2_psrep", [128, NH], F32)

    r_const = Res("const")
    r_negs = Res("negs")
    r_bd = Res("bd")
    r_etmp = Res("etmp")
    r_ddt = Res("ddt")
    r_qdec = Res("qdec")
    r_kdec = Res("kdec")
    r_kv = [Res(f"kv{h}") for h in range(NH)]
    r_kvbf = Res("kvbf")
    r_sqt = [Res(f"sqt{i}") for i in range(2)]
    r_skt = [Res(f"skt{i}") for i in range(2)]
    r_vt = [Res(f"vt{i}") for i in range(2)]
    r_qd = Res("qd")
    r_kd = Res("kd")
    r_kdt = Res("kdt")
    r_sT = [Res(f"sT{i}") for i in range(2)]
    r_oblk = [Res(f"oblk{i}") for i in range(2)]
    r_pssc = [Res(f"pssc{i}") for i in range(2)]
    r_pso = [Res(f"pso{i}") for i in range(2)]
    r_pstr = Res("pstr")
    r_pskv = Res("pskv")
    r_psrep = Res("psrep")

    sq3 = d["sq0"].ap().rearrange("(c p) t -> p c t", p=128)
    sk3 = d["sk0"].ap().rearrange("(c p) t -> p c t", p=128)
    v3 = d["v0"].ap().rearrange("(n p) j -> p n j", p=128)
    o3 = d["oraw0"].ap().rearrange("(n p) j -> p n j", p=128)
    ddt3 = ddt[:].rearrange("p (h s) -> p h s", h=NH)
    qdec3 = qdec[:].rearrange("p (h s) -> p h s", h=NH)
    kdec3 = kdec[:].rearrange("p (h s) -> p h s", h=NH)
    kv3 = kv[:].rearrange("p (h e) -> p h e", h=NH)
    sqt3 = [b[:].rearrange("p (h s) -> p h s", h=NH) for b in sqt]
    skt3 = [b[:].rearrange("p (h s) -> p h s", h=NH) for b in skt]
    vt3 = [b[:].rearrange("p (n j) -> p n j", n=2) for b in vt]
    ob3 = [b[:].rearrange("p (n j) -> p n j", n=2) for b in oblk]

    p.dma("sync", slope[:], d["slope_row"][:, :], writes=[r_const], label="slope")
    p.dma("sync", onesf[:], d["ones_row_f"][:, :], writes=[r_const])
    p.dma("sync", idxt[:].rearrange("p (n s) -> p n s", n=2),
          d["idxT"].ap().rearrange("(n p) s -> p n s", p=128), writes=[r_const])
    p.dma("sync", trilt[:].rearrange("p (n s) -> p n s", n=2),
          d["trilT"].ap().rearrange("(n p) s -> p n s", p=128), writes=[r_const])
    p.dma("sync", arrq[:], d["arrq"][:, :], writes=[r_const])
    p.dma("sync", arrk[:], d["arrk"][:, :], writes=[r_const])
    p.dma("gpsimd", identb[:], d["ident_bf"][:, :], writes=[r_const])

    p.op("tensor", lambda e: e.matmul(ps_rep[:], onesf[:, :], slope[0:1, :],
                                      start=True, stop=True),
         reads=[r_const], writes=[r_psrep], label="repslope")
    p.op("vector", lambda e: e.tensor_scalar(negs[:], ps_rep[:], -1.0, None, ALU.mult),
         reads=[r_psrep], writes=[r_negs], label="negs")
    p.op("scalar", lambda e: e.activation(bdrow[:], slope[:, :], AF.Exp, scale=-256.0),
         reads=[r_const], writes=[r_bd], label="bdrow")
    p.op("tensor", lambda e: e.matmul(ps_rep[:], onesf[:, :], bdrow[0:1, :],
                                      start=True, stop=True),
         reads=[r_bd, r_const], writes=[r_psrep], label="repbd")
    p.op("vector", lambda e: e.tensor_copy(bdrep[:], ps_rep[:]),
         reads=[r_psrep], writes=[r_bd], label="bdrep")
    for h in range(NH):
        for ss in range(2):
            p.op("scalar", lambda e, h=h, ss=ss: e.activation(
                etmp[:], idxt[:, ss * 256:(ss + 1) * 256], AF.Exp,
                scale=negs[:, h:h + 1]),
                reads=[r_const, r_negs], writes=[r_etmp], label=f"dde{h}.{ss}")
            p.op("vector", lambda e, h=h, ss=ss: e.tensor_tensor(
                ddt3[:, h, ss * 256:(ss + 1) * 256], etmp[:],
                trilt[:, ss * 256:(ss + 1) * 256], ALU.mult),
                reads=[r_etmp, r_const], writes=[r_ddt], label=f"ddm{h}.{ss}")
        p.op("scalar", lambda e, h=h: e.activation(
            qdec3[:, h, :], arrq[:], AF.Exp, scale=negs[:, h:h + 1]),
            reads=[r_const, r_negs], writes=[r_qdec], label=f"qdec{h}")
        p.op("scalar", lambda e, h=h: e.activation(
            kdec3[:, h, :], arrk[:], AF.Exp, scale=negs[:, h:h + 1]),
            reads=[r_const, r_negs], writes=[r_kdec], label=f"kdec{h}")
    p.op("vector", lambda e: e.memset(kv[:], 0.0), writes=[r_kv[0]],
         deps=[], label="kv0")
    for h in range(1, NH):
        r_kv[h].w = r_kv[0].w

    for blk in range(NB):
        bi = blk % 2
        p.dma("sync", sqt3[bi][:, :, :], sq3[:, :, blk * 256:(blk + 1) * 256],
              writes=[r_sqt[bi]], label=f"ldsq{blk}")
        p.dma("sync", skt3[bi][:, :, :], sk3[:, :, blk * 256:(blk + 1) * 256],
              writes=[r_skt[bi]], label=f"ldsk{blk}")
        p.dma("sync", vt3[bi][:, :, :], v3[:, blk * 2:blk * 2 + 2, :],
              writes=[r_vt[bi]], label=f"ldv{blk}")
        for h in range(NH):
            p.op("scalar", lambda e, h=h: e.copy(kvbf[:], kv3[:, h, :]),
                 reads=[r_kv[h]], writes=[r_kvbf], label=f"kvbf{blk}.{h}")
            p.op("gpsimd", lambda e, bi=bi, h=h: e.tensor_tensor(
                qd[:], sqt3[bi][:, h, :], qdec3[:, h, :], ALU.mult),
                reads=[r_sqt[bi], r_qdec], writes=[r_qd], label=f"qd{blk}.{h}")
            p.op("gpsimd", lambda e, bi=bi, h=h: e.tensor_tensor(
                kd[:], skt3[bi][:, h, :], kdec3[:, h, :], ALU.mult),
                reads=[r_skt[bi], r_kdec], writes=[r_kd], label=f"kd{blk}.{h}")
            for ss in range(2):
                p.op("tensor", lambda e, ss=ss: e.transpose(
                    ps_tr[:], kd[:, ss * 128:(ss + 1) * 128], identb[:]),
                    reads=[r_kd, r_const], writes=[r_pstr], label=f"ktr{blk}.{h}.{ss}")
                p.op("vector", lambda e, ss=ss: e.tensor_copy(
                    kdt[:, ss * 128:(ss + 1) * 128], ps_tr[:]),
                    reads=[r_pstr], writes=[r_kdt], label=f"ktre{blk}.{h}.{ss}")
            for ss in range(2):
                p.op("tensor", lambda e, bi=bi, h=h, ss=ss: e.matmul(
                    ps_sc[ss][:], skt3[bi][:, h, ss * 128:(ss + 1) * 128],
                    sqt3[bi][:, h, :], start=True, stop=True),
                    reads=[r_skt[bi], r_sqt[bi]], writes=[r_pssc[ss]],
                    label=f"sc{blk}.{h}.{ss}")
                p.op("vector", lambda e, h=h, ss=ss: e.tensor_tensor(
                    sT[ss][:], ps_sc[ss][:], ddt3[:, h, ss * 256:(ss + 1) * 256],
                    ALU.mult),
                    reads=[r_pssc[ss], r_ddt], writes=[r_sT[ss]],
                    label=f"scm{blk}.{h}.{ss}")
            obi = blk % 2
            for ms in range(2):
                p.op("tensor", lambda e, h=h, ms=ms: e.matmul(
                    ps_o[ms][:], qd[:, ms * 128:(ms + 1) * 128], kvbf[:],
                    start=True, stop=False),
                    reads=[r_qd, r_kvbf], writes=[r_pso[ms]], label=f"onv{blk}.{h}.{ms}")
                for ss in range(ms + 1):
                    last = ss == ms
                    p.op("tensor", lambda e, bi=bi, h=h, ms=ms, ss=ss, last=last: e.matmul(
                        ps_o[ms][:], sT[ss][:, ms * 128:(ms + 1) * 128],
                        vt3[bi][:, ss, h * 128:(h + 1) * 128],
                        start=False, stop=last),
                        reads=[r_sT[ss], r_vt[bi]], writes=[r_pso[ms]],
                        label=f"odg{blk}.{h}.{ms}.{ss}")
                p.op("scalar", lambda e, h=h, ms=ms, obi=obi: e.copy(
                    ob3[obi][:, ms, h * 128:(h + 1) * 128], ps_o[ms][:]),
                    reads=[r_pso[ms]], writes=[r_oblk[obi]], label=f"oev{blk}.{h}.{ms}")
            for ss in range(2):
                p.op("tensor", lambda e, bi=bi, h=h, ss=ss: e.matmul(
                    ps_kv[:], kdt[:, ss * 128:(ss + 1) * 128],
                    vt3[bi][:, ss, h * 128:(h + 1) * 128],
                    start=(ss == 0), stop=(ss == 1)),
                    reads=[r_kdt, r_vt[bi]], writes=[r_pskv], label=f"kvm{blk}.{h}.{ss}")
            p.op("vector", lambda e, h=h: e.tensor_scalar(
                kv3[:, h, :], kv3[:, h, :], bdrep[:, h:h + 1], None, ALU.mult),
                reads=[r_bd, r_kv[h]], writes=[r_kv[h]], label=f"kvs{blk}.{h}")
            p.op("vector", lambda e, h=h: e.tensor_tensor(
                kv3[:, h, :], kv3[:, h, :], ps_kv[:], ALU.add),
                reads=[r_pskv, r_kv[h]], writes=[r_kv[h]], label=f"kva{blk}.{h}")
        p.dma("sync", o3[:, blk * 2:blk * 2 + 2, :], ob3[blk % 2][:, :, :],
              reads=[r_oblk[blk % 2]], label=f"sto{blk}")


def build_p2(kind_inter="ExternalOutput"):
    nc = bass.Bass("TRN2", target_bir_lowering=False)
    d = {}
    d["sq0"] = nc.dram_tensor("sq0", [D, T], BF16, kind="ExternalInput")
    d["sk0"] = nc.dram_tensor("sk0", [D, T], BF16, kind="ExternalInput")
    d["v0"] = nc.dram_tensor("v0", [T, D], BF16, kind="ExternalInput")
    d["slope_row"] = nc.dram_tensor("slope_row", [1, NH], F32, kind="ExternalInput")
    d["idxT"] = nc.dram_tensor("idxT", [BLK, BLK], F32, kind="ExternalInput")
    d["trilT"] = nc.dram_tensor("trilT", [BLK, BLK], F32, kind="ExternalInput")
    d["arrq"] = nc.dram_tensor("arrq", [128, BLK], F32, kind="ExternalInput")
    d["arrk"] = nc.dram_tensor("arrk", [128, BLK], F32, kind="ExternalInput")
    d["ident_bf"] = nc.dram_tensor("ident_bf", [128, 128], BF16, kind="ExternalInput")
    d["ones_row_f"] = nc.dram_tensor("ones_row_f", [1, 128], F32, kind="ExternalInput")
    d["oraw0"] = nc.dram_tensor("oraw0", [T, D], BF16, kind=kind_inter)
    p = Prog(nc)
    sc = Scope(nc)
    emit_p2(p, d, sc)
    sc.close()
    p.finalize()
    return nc



def emit_p3(p, d, sc):
    nc = p.nc
    NG = T // 512

    woutb = sc.sb("p3_wout", [128, NC16 * D], BF16)
    identb = sc.sb("p3_identb", [128, 128], BF16)
    ot = [sc.sb(f"p3_ot{i}", [128, 4 * D], BF16) for i in range(2)]
    ut = [sc.sb(f"p3_ut{i}", [128, 4 * D], BF16) for i in range(2)]
    scr = sc.sb("p3_scr", [128, D], BF16)
    og = sc.sb("p3_og", [128, D], BF16)
    cols = sc.sb("p3_cols", [128, 4], F32)
    ogf = sc.sb("p3_ogf", [128, NC16 * 512], BF16)
    xt = [sc.sb(f"p3_xt{i}", [128, 512], F32) for i in range(2)]
    aev = [sc.sb(f"p3_aev{i}", [128, 512], F32) for i in range(2)]

    ps_mm = [sc.ps(f"p3_psmm{i}", [128, 512], F32) for i in range(2)]
    ps_tr = sc.ps("p3_pstr", [128, 128], BF16)

    r_const = Res("const")
    r_ot = [Res(f"ot{i}") for i in range(2)]
    r_ut = [Res(f"ut{i}") for i in range(2)]
    r_scr = Res("scr")
    r_og = Res("og")
    r_cols = Res("cols")
    r_ogf = Res("ogf")
    r_xt = [Res(f"xt{i}") for i in range(2)]
    r_aev = [Res(f"aev{i}") for i in range(2)]
    r_psmm = [Res(f"psmm{i}") for i in range(2)]
    r_pstr = Res("pstr")

    o3 = d["oraw0"].ap().rearrange("(n p) j -> p n j", p=128)
    u3 = d["u0"].ap().rearrange("(n p) j -> p n j", p=128)
    w3 = woutb[:].rearrange("p (c j) -> p c j", c=NC16)
    wr = d["woutT"].ap().rearrange("(c p) j -> p c j", p=128)
    xr = d["xT"].ap().rearrange("(c p) t -> p c t", p=128)
    xar = d["xa"].ap().rearrange("(c p) t -> p c t", p=128)
    ogf3 = ogf[:].rearrange("p (c t) -> p c t", c=NC16)
    ot3 = [b[:].rearrange("p (n j) -> p n j", n=4) for b in ot]
    ut3 = [b[:].rearrange("p (n j) -> p n j", n=4) for b in ut]

    p.dma("gpsimd", identb[:], d["ident_bf"][:, :], writes=[r_const])
    for piece in range(4):
        p.dma("gpsimd", w3[:, piece * 4:(piece + 1) * 4, :],
              wr[:, piece * 4:(piece + 1) * 4, :], writes=[r_const], label=f"ldwout{piece}")

    for g in range(NG):
        bi = g % 2
        p.dma("sync", ot3[bi][:, :, :], o3[:, g * 4:(g + 1) * 4, :],
              writes=[r_ot[bi]], label=f"ldo{g}")
        p.dma("sync", ut3[bi][:, :, :], u3[:, g * 4:(g + 1) * 4, :],
              writes=[r_ut[bi]], label=f"ldu{g}")
        for s in range(4):
            p.op("scalar", lambda e, bi=bi, s=s: e.activation(
                scr[:], ot3[bi][:, s, :], AF.Square, accum_out=cols[:, 0:1]),
                reads=[r_ot[bi]], writes=[r_scr, r_cols], label=f"ssq{g}.{s}")
            p.op("vector", lambda e: e.tensor_scalar(
                cols[:, 1:2], cols[:, 0:1], 1.0 / D, EPS0, ALU.mult, ALU.add),
                reads=[r_cols], writes=[r_cols])
            p.op("scalar", lambda e: e.activation(cols[:, 2:3], cols[:, 1:2], AF.Sqrt),
                 reads=[r_cols], writes=[r_cols], label=f"sqrt{g}.{s}")
            p.op("vector", lambda e: e.reciprocal(cols[:, 3:4], cols[:, 2:3]),
                 reads=[r_cols], writes=[r_cols])
            p.op("vector", lambda e, bi=bi, s=s: e.tensor_tensor(
                og[:], ot3[bi][:, s, :], ut3[bi][:, s, :], ALU.mult),
                reads=[r_ot[bi], r_ut[bi]], writes=[r_og], label=f"ogu{g}.{s}")
            p.op("vector", lambda e: e.tensor_scalar(
                og[:], og[:], cols[:, 3:4], None, ALU.mult),
                reads=[r_cols, r_og], writes=[r_og], label=f"ogr{g}.{s}")
            for c in range(NC16):
                p.op("tensor", lambda e, c=c: e.transpose(
                    ps_tr[:], og[:, c * 128:(c + 1) * 128], identb[:]),
                    reads=[r_og, r_const], writes=[r_pstr], label=f"tr{g}.{s}.{c}")
                p.op("vector", lambda e, c=c, s=s: e.tensor_copy(
                    ogf3[:, c, s * 128:(s + 1) * 128], ps_tr[:]),
                    reads=[r_pstr], writes=[r_ogf], label=f"tre{g}.{s}.{c}")
        for j in range(NC16):
            pi = j % 2
            for c in range(NC16):
                p.op("tensor", lambda e, j=j, c=c, pi=pi: e.matmul(
                    ps_mm[pi][:], w3[:, c, j * 128:(j + 1) * 128], ogf3[:, c, :],
                    start=(c == 0), stop=(c == NC16 - 1)),
                    reads=[r_const, r_ogf], writes=[r_psmm[pi]], label=f"wo{g}.{j}")
            p.dma("sync", xt[pi][:], xr[:, j, g * 512:(g + 1) * 512],
                  writes=[r_xt[pi]], label=f"ldx{g}.{j}")
            p.op("vector", lambda e, pi=pi: e.tensor_tensor(
                aev[pi][:], ps_mm[pi][:], xt[pi][:], ALU.add),
                reads=[r_psmm[pi], r_xt[pi]], writes=[r_aev[pi]], label=f"aev{g}.{j}")
            p.dma("sync", xar[:, j, g * 512:(g + 1) * 512], aev[pi][:],
                  reads=[r_aev[pi]], label=f"stxa{g}.{j}")


def build_p3(kind_inter="ExternalOutput"):
    nc = bass.Bass("TRN2", target_bir_lowering=False)
    d = {}
    d["oraw0"] = nc.dram_tensor("oraw0", [T, D], BF16, kind="ExternalInput")
    d["u0"] = nc.dram_tensor("u0", [T, D], BF16, kind="ExternalInput")
    d["woutT"] = nc.dram_tensor("woutT", [D, D], F32, kind="ExternalInput")
    d["xT"] = nc.dram_tensor("xT", [D, T], F32, kind="ExternalInput")
    d["ident_bf"] = nc.dram_tensor("ident_bf", [128, 128], BF16, kind="ExternalInput")
    d["xa"] = nc.dram_tensor("xa", [D, T], F32, kind=kind_inter)
    p = Prog(nc)
    sc = Scope(nc)
    emit_p3(p, d, sc)
    sc.close()
    p.finalize()
    return nc



class QuantKit:

    def __init__(self, p, d, tag, sc, nsets=2):
        nc = p.nc
        self.sets = []
        for s in range(nsets):
            bufs = dict(
                wabsf=sc.sb(f"{tag}_wabsf{s}", [128, 512], F32),
                wabsb=sc.sb(f"{tag}_wabsb{s}", [128, 512], BF16),
                sgn=sc.sb(f"{tag}_sgn{s}", [128, 512], F32),
                sbf=sc.sb(f"{tag}_sbf{s}", [1, 1024], BF16),
                thr=sc.sb(f"{tag}_thr{s}", [1, 1024], F32),
                trep=sc.sb(f"{tag}_trep{s}", [128, 512], F32),
                srep=sc.sb(f"{tag}_srep{s}", [128, 512], F32),
                mask=sc.sb(f"{tag}_mask{s}", [128, 512], F32),
                ps_r=sc.ps(f"{tag}_psr{s}", [128, 512], F32),
            )
            res = {k: Res(f"{tag}{k}{s}") for k in
                   ("wabsf", "wabsb", "sgn", "sbf", "thr", "trep", "srep",
                    "mask", "psr")}
            self.sets.append((bufs, res))
        self.onesf = sc.sb(f"{tag}_onesf", [1, 128], F32)
        self.onesbr = sc.sb(f"{tag}_onesbr", [1, 128], BF16)
        self.onesbc = sc.sb(f"{tag}_onesbc", [128, 1], BF16)
        self.gsel = sc.sb(f"{tag}_gsel", [128, 128], BF16)
        self.r_const = Res(f"{tag}const")
        p.dma("sync", self.onesf[:], d["ones_row_f"][:, :], writes=[self.r_const])
        p.dma("gpsimd", self.onesbr[:], d["ones_row_f"][:, :], writes=[self.r_const])
        p.dma("gpsimd", self.onesbc[:], d["ones_col_bf"][:, :], writes=[self.r_const])
        p.dma("gpsimd", self.gsel[:], d["gsel_bf"][:, :], writes=[self.r_const])
        self.n = 0

    def quant(self, p, wtile_ap, r_wt, label):
        bufs, rr = self.sets[self.n % len(self.sets)]
        self.n += 1
        rc = self.r_const
        p.op("scalar", lambda e: e.activation(bufs["wabsf"][:], wtile_ap, AF.Abs),
             reads=[r_wt], writes=[rr["wabsf"]], label=label + "af")
        p.op("scalar", lambda e: e.activation(bufs["wabsb"][:], wtile_ap, AF.Abs),
             reads=[r_wt], writes=[rr["wabsb"]], label=label + "ab")
        p.op("scalar", lambda e: e.activation(bufs["sgn"][:], wtile_ap, AF.Sign),
             reads=[r_wt], writes=[rr["sgn"]], label=label + "sg")
        p.op("tensor", lambda e: e.matmul(bufs["ps_r"][:], self.gsel[:],
                                          bufs["wabsb"][:], start=True, stop=True),
             reads=[rr["wabsb"], rc], writes=[rr["psr"]], label=label + "gs")
        p.op("vector", lambda e: e.tensor_scalar(
            bufs["sbf"][:, 0:512], bufs["ps_r"][0:1, :], 1.0 / 64, 1e-8,
            ALU.mult, ALU.max),
            reads=[rr["psr"]], writes=[rr["sbf"]], label=label + "sc0")
        p.op("vector", lambda e: e.tensor_scalar(
            bufs["sbf"][:, 512:1024], bufs["ps_r"][64:65, :], 1.0 / 64, 1e-8,
            ALU.mult, ALU.max),
            reads=[rr["psr"]], writes=[rr["sbf"]], label=label + "sc1")
        p.op("vector", lambda e: e.tensor_scalar(
            bufs["thr"][:], bufs["sbf"][:], THRC, None, ALU.mult),
            reads=[rr["sbf"]], writes=[rr["thr"]], label=label + "th")
        for gg in range(2):
            p.op("tensor", lambda e, gg=gg: e.matmul(
                bufs["ps_r"][gg * 64:(gg + 1) * 64, :], self.onesbr[0:1, 0:64],
                bufs["sbf"][0:1, gg * 512:(gg + 1) * 512], start=True, stop=True),
                reads=[rr["sbf"], rc], writes=[rr["psr"]], label=label + f"rs{gg}")
        p.op("vector", lambda e: e.tensor_copy(bufs["srep"][:], bufs["ps_r"][:]),
             reads=[rr["psr"]], writes=[rr["srep"]], label=label + "se")
        for gg in range(2):
            p.op("tensor", lambda e, gg=gg: e.matmul(
                bufs["ps_r"][gg * 64:(gg + 1) * 64, :], self.onesf[0:1, 0:64],
                bufs["thr"][0:1, gg * 512:(gg + 1) * 512], start=True, stop=True),
                reads=[rr["thr"], rc], writes=[rr["psr"]], label=label + f"rt{gg}")
        p.op("vector", lambda e: e.tensor_copy(bufs["trep"][:], bufs["ps_r"][:]),
             reads=[rr["psr"]], writes=[rr["trep"]], label=label + "te")
        p.op("vector", lambda e: e.tensor_tensor(
            bufs["mask"][:], bufs["wabsf"][:], bufs["trep"][:], ALU.is_gt),
            reads=[rr["wabsf"], rr["trep"]], writes=[rr["mask"]], label=label + "mk")
        p.op("gpsimd", lambda e: e.tensor_tensor(
            bufs["mask"][:], bufs["mask"][:], bufs["sgn"][:], ALU.mult),
            reads=[rr["sgn"], rr["mask"]], writes=[rr["mask"]], label=label + "q")
        p.op("vector", lambda e: e.tensor_tensor(
            wtile_ap, bufs["mask"][:], bufs["srep"][:], ALU.mult),
            reads=[rr["mask"], rr["srep"]], writes=[r_wt], label=label + "wh")


def emit_p0quant(p, d, dr, sc):
    nc = p.nc
    qk = QuantKit(p, d, "p0", sc, nsets=2)
    wt = [sc.sb(f"p0_wt{i}", [128, 512], BF16) for i in range(4)]
    r_wt = [Res(f"p0wt{i}") for i in range(4)]
    wgur = d["wguT"].ap().rearrange("(c p) j -> p c j", p=128)
    wguqr = d["wguq"].ap().rearrange("(c p) j -> p c j", p=128)
    wpr = d["wprojT"].ap().rearrange("(c p) j -> p c j", p=128)
    wpqr = d["wprojq"].ap().rearrange("(c p) j -> p c j", p=128)
    k = 0
    for g in range(32):
        for c in range(NC16):
            b = k % 4
            k += 1
            p.dma("gpsimd", wt[b][:], wgur[:, c, g * 512:(g + 1) * 512],
                  writes=[r_wt[b]], label=f"p0lg{g}.{c}")
            qk.quant(p, wt[b][:], r_wt[b], f"p0q{g}.{c}")
            p.dma("sync", wguqr[:, c, g * 512:(g + 1) * 512], wt[b][:],
                  reads=[r_wt[b]], writes=[dr["wguq"]], label=f"p0sg{g}.{c}")
    for g in range(4):
        for c in range(HF0 // 128):
            b = k % 4
            k += 1
            p.dma("gpsimd", wt[b][:], wpr[:, c, g * 512:(g + 1) * 512],
                  writes=[r_wt[b]], label=f"p0lp{g}.{c}")
            qk.quant(p, wt[b][:], r_wt[b], f"p0p{g}.{c}")
            p.dma("sync", wpqr[:, c, g * 512:(g + 1) * 512], wt[b][:],
                  reads=[r_wt[b]], writes=[dr["wprojq"]], label=f"p0sp{g}.{c}")


def emit_p4a(p, d, dr, sc):
    nc = p.nc
    sxn = sc.sb("p4a_sxn", [128, NC16 * TH], BF16)
    wg = sc.sb("p4a_wg", [128, NC16 * 512], BF16)
    wu = sc.sb("p4a_wu", [128, NC16 * 512], BF16)
    sgv = sc.sb("p4a_sgv", [128, 512], BF16)
    suv = sc.sb("p4a_suv", [128, 512], BF16)
    shv = [sc.sb(f"p4a_shv{i}", [128, 512], BF16) for i in range(2)]
    ps_g = [sc.ps(f"p4a_psg{i}", [128, 512], F32) for i in range(2)]
    ps_u = [sc.ps(f"p4a_psu{i}", [128, 512], F32) for i in range(2)]

    r_xn = Res("xn")
    r_wg = [Res(f"wgc{c}") for c in range(NC16)]
    r_wu = [Res(f"wuc{c}") for c in range(NC16)]
    r_sgv = Res("sgv")
    r_suv = Res("suv")
    r_shv = [Res(f"shv{i}") for i in range(2)]
    r_psg = [Res(f"psg{i}") for i in range(2)]
    r_psu = [Res(f"psu{i}") for i in range(2)]

    xnr = d["xn0"].ap().rearrange("(c p) t -> p c t", p=128)
    wgur = d["wguq"].ap().rearrange("(c p) j -> p c j", p=128)
    hr = d["h0"].ap().rearrange("(c p) t -> p c t", p=128)
    sxn3 = sxn[:].rearrange("p (c t) -> p c t", c=NC16)
    wg3 = wg[:].rearrange("p (c j) -> p c j", c=NC16)
    wu3 = wu[:].rearrange("p (c j) -> p c j", c=NC16)

    NTH = TH // 512
    for h in range(2):
        for piece in range(4):
            p.dma("gpsimd", sxn3[:, piece * 4:(piece + 1) * 4, :],
                  xnr[:, piece * 4:(piece + 1) * 4, h * TH:(h + 1) * TH],
                  reads=[dr["xn0"]], writes=[r_xn], label=f"ldxn{h}")
        for g in range(16):
            for c in range(NC16):
                p.dma("sync", wg3[:, c, :], wgur[:, c, g * 512:(g + 1) * 512],
                      reads=[dr["wguq"]], writes=[r_wg[c]], label=f"ldwg{h}.{g}.{c}")
                p.dma("sync", wu3[:, c, :],
                      wgur[:, c, HF0 + g * 512:HF0 + (g + 1) * 512],
                      reads=[dr["wguq"]], writes=[r_wu[c]], label=f"ldwu{h}.{g}.{c}")
            for jj in range(4):
                for t in range(NTH):
                    i = jj * NTH + t
                    pi = i % 2
                    for c in range(NC16):
                        p.op("tensor", lambda e, c=c, jj=jj, t=t, pi=pi: e.matmul(
                            ps_g[pi][:], wg3[:, c, jj * 128:(jj + 1) * 128],
                            sxn3[:, c, t * 512:(t + 1) * 512],
                            start=(c == 0), stop=(c == NC16 - 1)),
                            reads=[r_wg[c], r_xn], writes=[r_psg[pi]],
                            label=f"gmm{h}.{g}.{i}")
                    for c in range(NC16):
                        p.op("tensor", lambda e, c=c, jj=jj, t=t, pi=pi: e.matmul(
                            ps_u[pi][:], wu3[:, c, jj * 128:(jj + 1) * 128],
                            sxn3[:, c, t * 512:(t + 1) * 512],
                            start=(c == 0), stop=(c == NC16 - 1)),
                            reads=[r_wu[c], r_xn], writes=[r_psu[pi]],
                            label=f"umm{h}.{g}.{i}")
                    p.op("scalar", lambda e, pi=pi: e.activation(
                        sgv[:], ps_g[pi][:], AF.Silu),
                        reads=[r_psg[pi]], writes=[r_sgv], label=f"gsil{h}.{g}.{i}")
                    p.op("vector", lambda e, pi=pi: e.tensor_copy(suv[:], ps_u[pi][:]),
                         reads=[r_psu[pi]], writes=[r_suv], label=f"uev{h}.{g}.{i}")
                    hb = i % 2
                    p.op("vector", lambda e, hb=hb: e.tensor_tensor(
                        shv[hb][:], sgv[:], suv[:], ALU.mult),
                        reads=[r_sgv, r_suv], writes=[r_shv[hb]], label=f"hm{h}.{g}.{i}")
                    p.dma("sync",
                          hr[:, g * 4 + jj, h * TH + t * 512:h * TH + (t + 1) * 512],
                          shv[hb][:], reads=[r_shv[hb]], writes=[dr["h0"]],
                          label=f"sth{h}.{g}.{i}")


def emit_p4b(p, d, dr, sc):
    nc = p.nc
    NHC = HF0 // 128
    wpq = sc.sb("p4b_wpq", [128, NHC * 512], BF16)
    sh = sc.sb("p4b_sh", [128, NHC * 512], BF16)
    shsq = [sc.sb(f"p4b_shsq{i}", [128, 512], BF16) for i in range(2)]
    rows = sc.sb("p4b_rows", [1, 3 * 512], F32)
    rrep = sc.sb("p4b_rrep", [128, 512], F32)
    xat = sc.sb("p4b_xat", [128, 512], F32)
    x1t = [sc.sb(f"p4b_x1t{i}", [128, 512], F32) for i in range(2)]
    ps_mm = [sc.ps(f"p4b_psmm{i}", [128, 512], F32) for i in range(2)]
    ps_ss = sc.ps("p4b_psss", [1, 512], F32)
    ps_rep4b = sc.ps("p4b_psrep", [128, 512], F32)
    onesbc4b = sc.sb("p4b_onesbc", [128, 1], BF16)
    onesf4b = sc.sb("p4b_onesf", [1, 128], F32)
    r_c4b = Res("c4b")
    r_psrep4b = Res("psrep4b")

    r_wpq = [Res(f"wpq{c}") for c in range(NHC)]
    r_sh = Res("sh")
    r_shsq = [Res(f"shsq{i}") for i in range(2)]
    r_rows = Res("rows")
    r_rrep = Res("rrep")
    r_xat = Res("xat")
    r_x1t = [Res(f"x1t{i}") for i in range(2)]
    r_psmm = [Res(f"psmm{i}") for i in range(2)]
    r_psss = Res("psss")

    wpr = d["wprojq"].ap().rearrange("(c p) j -> p c j", p=128)
    hr = d["h0"].ap().rearrange("(c p) t -> p c t", p=128)
    xar = d["xa"].ap().rearrange("(c p) t -> p c t", p=128)
    x1r = d["x1"].ap().rearrange("(c p) t -> p c t", p=128)
    hrr = d["hrstd"].ap()
    sh3 = sh[:].rearrange("p (c t) -> p c t", c=NHC)
    wpq3 = wpq[:].rearrange("p (c j) -> p c j", c=NHC)

    p.dma("gpsimd", onesbc4b[:], d["ones_col_bf"][:, :], writes=[r_c4b])
    p.dma("sync", onesf4b[:], d["ones_row_f"][:, :], writes=[r_c4b])
    NT = T // 512
    for q in range(4):
        for c in range(NHC):
            p.dma("sync", wpq3[:, c, :], wpr[:, c, q * 512:(q + 1) * 512],
                  reads=[dr["wprojq"]], writes=[r_wpq[c]], label=f"ldwp{q}.{c}")
        for t in range(NT):
            for piece in range(8):
                p.dma("sync", sh3[:, piece * 8:(piece + 1) * 8, :],
                      hr[:, piece * 8:(piece + 1) * 8, t * 512:(t + 1) * 512],
                      reads=[dr["h0"]], writes=[r_sh], label=f"ldh{q}.{t}")
            if q == 0:
                for c in range(NHC):
                    si = c % 2
                    p.op("scalar", lambda e, c=c, si=si: e.activation(
                        shsq[si][:], sh3[:, c, :], AF.Square),
                        reads=[r_sh], writes=[r_shsq[si]], label=f"hsq{t}.{c}")
                    p.op("tensor", lambda e, c=c, si=si: e.matmul(
                        ps_ss[:], onesbc4b[:, 0:1], shsq[si][:],
                        start=(c == 0), stop=(c == NHC - 1)),
                        reads=[r_shsq[si], r_c4b], writes=[r_psss],
                        label=f"hsm{t}.{c}")
                p.op("vector", lambda e: e.tensor_scalar(
                    rows[:, 0:512], ps_ss[0:1, :], 1.0 / HF0, FEPS, ALU.mult, ALU.add),
                    reads=[r_psss], writes=[r_rows])
                p.op("scalar", lambda e: e.activation(
                    rows[:, 512:1024], rows[:, 0:512], AF.Sqrt),
                    reads=[r_rows], writes=[r_rows], label=f"hrs{t}")
                p.op("vector", lambda e: e.reciprocal(rows[:, 1024:1536], rows[:, 512:1024]),
                     reads=[r_rows], writes=[r_rows])
                p.op("tensor", lambda e: e.matmul(
                    ps_rep4b[:], onesf4b[:, :], rows[0:1, 1024:1536],
                    start=True, stop=True),
                    reads=[r_rows, r_c4b], writes=[r_psrep4b], label=f"hrr{t}")
                p.op("vector", lambda e: e.tensor_copy(rrep[:], ps_rep4b[:]),
                     reads=[r_psrep4b], writes=[r_rrep], label=f"hrre{t}")
                p.dma("sync", hrr[:, t * 512:(t + 1) * 512], rrep[:],
                      reads=[r_rrep], writes=[dr["hrstd"]], label=f"sthr{t}")
            else:
                p.dma("sync", rrep[:], hrr[:, t * 512:(t + 1) * 512],
                      reads=[dr["hrstd"]], writes=[r_rrep], label=f"ldhr{q}.{t}")
            for j in range(4):
                pi = j % 2
                for c in range(NHC):
                    p.op("tensor", lambda e, j=j, c=c, pi=pi: e.matmul(
                        ps_mm[pi][:], wpq3[:, c, j * 128:(j + 1) * 128], sh3[:, c, :],
                        start=(c == 0), stop=(c == NHC - 1)),
                        reads=[r_wpq[c], r_sh], writes=[r_psmm[pi]],
                        label=f"pmm{q}.{t}.{j}")
                jr = q * 4 + j
                p.dma("sync", xat[:], xar[:, jr, t * 512:(t + 1) * 512],
                      reads=[dr["xa"]], writes=[r_xat], label=f"ldxa{q}.{t}.{j}")
                p.op("vector", lambda e, pi=pi: e.tensor_tensor(
                    x1t[pi][:], ps_mm[pi][:], rrep[:], ALU.mult),
                    reads=[r_psmm[pi], r_rrep], writes=[r_x1t[pi]], label=f"x1m{q}.{t}.{j}")
                p.op("vector", lambda e, pi=pi: e.tensor_tensor(
                    x1t[pi][:], x1t[pi][:], xat[:], ALU.add),
                    reads=[r_xat, r_x1t[pi]], writes=[r_x1t[pi]], label=f"x1a{q}.{t}.{j}")
                p.dma("sync", x1r[:, jr, t * 512:(t + 1) * 512], x1t[pi][:],
                      reads=[r_x1t[pi]], writes=[dr["x1"]], label=f"stx1{q}.{t}.{j}")


def build_p4(kind_inter="ExternalOutput"):
    nc = bass.Bass("TRN2", target_bir_lowering=False)
    d = {}
    d["xn0"] = nc.dram_tensor("xn0", [D, T], BF16, kind="ExternalInput")
    d["wguT"] = nc.dram_tensor("wguT", [D, 2 * HF0], F32, kind="ExternalInput")
    d["wprojT"] = nc.dram_tensor("wprojT", [HF0, D], F32, kind="ExternalInput")
    d["xa"] = nc.dram_tensor("xa", [D, T], F32, kind="ExternalInput")
    d["ones_row_f"] = nc.dram_tensor("ones_row_f", [1, 128], F32, kind="ExternalInput")
    d["ones_col_bf"] = nc.dram_tensor("ones_col_bf", [128, 1], BF16, kind="ExternalInput")
    d["gsel_bf"] = nc.dram_tensor("gsel_bf", [128, 128], BF16, kind="ExternalInput")
    d["h0"] = nc.dram_tensor("h0", [HF0, T], BF16, kind="Internal")
    d["hrstd"] = nc.dram_tensor("hrstd", [128, T], F32, kind="Internal")
    d["x1"] = nc.dram_tensor("x1", [D, T], F32, kind=kind_inter)
    dr = {k: Res("dram_" + k) for k in d}
    p = Prog(nc)
    sc = Scope(nc)
    emit_p4a(p, d, dr, sc)
    sc.close()
    p.barrier()
    sc2 = Scope(nc)
    emit_p4b(p, d, dr, sc2)
    sc2.close()
    p.finalize()
    return nc



def emit_p5(p, d, dr, sc):
    nc = p.nc
    NT = T // 512
    NTH = NT // 2

    sx = [sc.sb(f"p5_sx{i}", [128, NC16 * 512], F32) for i in range(2)]
    sxbf = sc.sb("p5_sxbf", [128, NC16 * 512], BF16)
    ssq2 = [sc.sb(f"p5_ssq{i}", [128, 512], BF16) for i in range(2)]
    onesb = sc.sb("p5_onesb", [128, 1], BF16)
    onesf = sc.sb("p5_onesf", [1, 128], F32)
    rows = sc.sb("p5_rows", [1, 6 * 512], F32)
    mrep = sc.sb("p5_mrep", [128, 512], F32)
    rrep = sc.sb("p5_rrep", [128, 512], F32)
    tmpf = sc.sb("p5_tmpf", [128, 512], F32)
    xnf = sc.sb("p5_xnf", [128, 512], F32)
    sgw = sc.sb("p5_sgw", [128, NC16 * 8], F32)
    lgacc = sc.sb("p5_lgacc", [128, 32], F32)
    redc = sc.sb("p5_redc", [128, 1], F32)
    sxn1 = sc.sb("p5_sxn1", [128, NC16], F32)
    sxn = sc.sb("p5_sxn", [128, NC16 * TH], BF16)
    swg = [sc.sb(f"p5_swg{i}", [128, NC16 * 512], BF16) for i in range(2)]
    sev = [sc.sb(f"p5_sev{i}", [128, 512], BF16) for i in range(4)]
    ea = sc.sb("p5_ea", [128, 512], F32)
    eb = sc.sb("p5_eb", [128, 512], F32)

    ps_mm = [sc.ps(f"p5_psmm{i}", [128, 512], F32) for i in range(2)]
    ps_s1 = sc.ps("p5_pss1", [1, 512], F32)
    ps_s2 = sc.ps("p5_pss2", [1, 512], F32)
    ps_rep = sc.ps("p5_psrep", [128, 512], F32)
    ps_lg = sc.ps("p5_pslg", [128, 32], F32)

    r_sx = [Res("sx0")]
    r_sx.append(r_sx[0])
    r_sxbf = Res("sxbf")
    r_ssq2 = [Res(f"ssq{i}") for i in range(2)]
    r_ones = Res("ones")
    r_rows = Res("rows")
    r_reps = Res("reps")
    r_tmpf = Res("tmpf")
    r_xnf = Res("xnf")
    r_lgacc = Res("lgacc")
    r_red = Res("red")
    r_sxn1 = Res("sxn1")
    r_xn = Res("xn")
    r_wg = [Res(f"wg{i}") for i in range(2)]
    r_ev = [Res(f"ev{i}") for i in range(4)]
    r_ea = Res("ea")
    r_eb = Res("eb")
    r_psmm = [Res(f"psmm{i}") for i in range(2)]
    r_ps1 = Res("ps1")
    r_ps2 = Res("ps2")
    r_psrep = Res("psrep")
    r_pslg = Res("pslg")

    xr = d["x1"].ap().rearrange("(c p) t -> p c t", p=128)
    xnr = d["xn1"].ap().rearrange("(c p) t -> p c t", p=128)
    gwr = d["gatewT"].ap().rearrange("(c p) e -> p c e", p=128)
    lgr = d["logits1"].ap().rearrange("(n p) e -> p n e", p=128)

    p.dma("sync", onesf[:], d["ones_row_f"][:, :], writes=[r_ones])
    p.dma("gpsimd", onesb[:], d["ones_col_bf"][:, :], writes=[r_ones])
    p.dma("sync", sgw[:].rearrange("p (c e) -> p c e", c=NC16), gwr[:, :, :],
          writes=[r_ones], label="ldgw")
    p.op("vector", lambda e: e.memset(sxn1[:], 0.0), writes=[r_sxn1], label="sxn1z")

    sx3 = [b[:].rearrange("p (c t) -> p c t", c=NC16) for b in sx]
    sxb3 = sxbf[:].rearrange("p (c t) -> p c t", c=NC16)
    sxn3 = sxn[:].rearrange("p (c t) -> p c t", c=NC16)
    sgw3 = sgw[:].rearrange("p (c e) -> p c e", c=NC16)
    swg3 = [b[:].rearrange("p (c j) -> p c j", c=NC16) for b in swg]

    for h in range(2):
        for tc in range(NTH):
            tq = h * NTH + tc
            bi = tq % 2
            for piece in range(4):
                p.dma("sync", sx3[bi][:, piece * 4:(piece + 1) * 4, :],
                      xr[:, piece * 4:(piece + 1) * 4, tq * 512:(tq + 1) * 512],
                      reads=[dr["x1"]], writes=[r_sx[bi]], label=f"ldx{tq}")
                p.dma("gpsimd", sxb3[:, piece * 4:(piece + 1) * 4, :],
                      xr[:, piece * 4:(piece + 1) * 4, tq * 512:(tq + 1) * 512],
                      reads=[dr["x1"]], writes=[r_sxbf], label=f"ldxb{tq}")
            for c in range(NC16):
                p.op("tensor", lambda e, c=c: e.matmul(
                    ps_s1[:], onesb[:, 0:1], sxb3[:, c, :],
                    start=(c == 0), stop=(c == NC16 - 1)),
                    reads=[r_ones, r_sxbf], writes=[r_ps1], label=f"mms1_{tq}")
            for c in range(NC16):
                sq = c % 2
                p.op("scalar", lambda e, bi=bi, c=c, sq=sq: e.activation(
                    ssq2[sq][:], sx3[bi][:, c, :], AF.Square),
                    reads=[r_sx[bi]], writes=[r_ssq2[sq]], label=f"sq{tq}.{c}")
                p.op("tensor", lambda e, c=c, sq=sq: e.matmul(
                    ps_s2[:], onesb[:, 0:1], ssq2[sq][:],
                    start=(c == 0), stop=(c == NC16 - 1)),
                    reads=[r_ones, r_ssq2[sq]], writes=[r_ps2], label=f"mms2_{tq}.{c}")
            R = rows
            p.op("vector", lambda e: e.tensor_scalar(R[:, 0:512], ps_s1[0:1, :], 1.0 / D, None, ALU.mult),
                 reads=[r_ps1], writes=[r_rows], label=f"mrow{tq}")
            p.op("vector", lambda e: e.tensor_scalar(R[:, 512:1024], ps_s2[0:1, :], 1.0 / D, None, ALU.mult),
                 reads=[r_ps2], writes=[r_rows], label=f"msq{tq}")
            p.op("vector", lambda e: e.tensor_tensor(R[:, 1024:1536], R[:, 0:512], R[:, 0:512], ALU.mult),
                 reads=[r_rows], writes=[r_rows])
            p.op("vector", lambda e: e.tensor_tensor(R[:, 1024:1536], R[:, 512:1024], R[:, 1024:1536], ALU.subtract),
                 reads=[r_rows], writes=[r_rows])
            p.op("vector", lambda e: e.tensor_scalar(R[:, 1024:1536], R[:, 1024:1536], EPSL, None, ALU.add),
                 reads=[r_rows], writes=[r_rows])
            p.op("scalar", lambda e: e.activation(R[:, 1536:2048], R[:, 1024:1536], AF.Sqrt),
                 reads=[r_rows], writes=[r_rows], label=f"sqrt{tq}")
            p.op("vector", lambda e: e.reciprocal(R[:, 2048:2560], R[:, 1536:2048]),
                 reads=[r_rows], writes=[r_rows])
            p.op("vector", lambda e: e.tensor_tensor(R[:, 2560:3072], R[:, 0:512], R[:, 2048:2560], ALU.mult),
                 reads=[r_rows], writes=[r_rows])
            p.op("tensor", lambda e: e.matmul(ps_rep[:], onesf[:, :], R[0:1, 2048:2560],
                                              start=True, stop=True),
                 reads=[r_ones, r_rows], writes=[r_psrep], label=f"reprstd{tq}")
            p.op("vector", lambda e: e.tensor_copy(rrep[:], ps_rep[:]),
                 reads=[r_psrep], writes=[r_reps])
            p.op("tensor", lambda e: e.matmul(ps_rep[:], onesf[:, :], R[0:1, 2560:3072],
                                              start=True, stop=True),
                 reads=[r_ones, r_rows], writes=[r_psrep], label=f"repmr{tq}")
            p.op("vector", lambda e: e.tensor_copy(mrep[:], ps_rep[:]),
                 reads=[r_psrep], writes=[r_reps])
            for c in range(NC16):
                p.op("vector", lambda e, bi=bi, c=c: e.tensor_tensor(
                    tmpf[:], sx3[bi][:, c, :], rrep[:], ALU.mult),
                    reads=[r_sx[bi], r_reps], writes=[r_tmpf])
                p.op("vector", lambda e, c=c: e.tensor_tensor(
                    xnf[:], tmpf[:], mrep[:], ALU.subtract),
                    reads=[r_tmpf, r_reps], writes=[r_xnf], label=f"xnf{tq}.{c}")
                p.op("vector", lambda e, c=c, tc=tc: e.tensor_copy(
                    sxn3[:, c, tc * 512:(tc + 1) * 512], xnf[:]),
                    reads=[r_xnf], writes=[r_xn], label=f"xn{tq}.{c}")
                for tsub in range(4):
                    p.op("tensor", lambda e, c=c, tsub=tsub: e.matmul(
                        ps_lg[:, tsub * 8:(tsub + 1) * 8],
                        xnf[:, tsub * 128:(tsub + 1) * 128], sgw3[:, c, :],
                        start=True, stop=True),
                        reads=[r_xnf, r_ones], writes=[r_pslg],
                        label=f"lg{tq}.{c}.{tsub}")
                    if c == 0:
                        p.op("vector", lambda e, tsub=tsub: e.tensor_copy(
                            lgacc[:, tsub * 8:(tsub + 1) * 8],
                            ps_lg[:, tsub * 8:(tsub + 1) * 8]),
                            reads=[r_pslg], writes=[r_lgacc],
                            label=f"lga{tq}.{c}.{tsub}")
                    else:
                        p.op("vector", lambda e, tsub=tsub: e.tensor_tensor(
                            lgacc[:, tsub * 8:(tsub + 1) * 8],
                            lgacc[:, tsub * 8:(tsub + 1) * 8],
                            ps_lg[:, tsub * 8:(tsub + 1) * 8], ALU.add),
                            reads=[r_pslg, r_lgacc], writes=[r_lgacc],
                            label=f"lga{tq}.{c}.{tsub}")
                p.op("vector", lambda e, c=c, tc=tc: e.reduce_sum(
                    redc[:, 0:1], sxn3[:, c, tc * 512:(tc + 1) * 512], axis=AX.X),
                    reads=[r_xn], writes=[r_red], label=f"red{tq}.{c}")
                p.op("vector", lambda e, c=c: e.tensor_tensor(
                    sxn1[:, c:c + 1], sxn1[:, c:c + 1], redc[:, 0:1], ALU.add),
                    reads=[r_red, r_sxn1], writes=[r_sxn1], label=f"reda{tq}.{c}")
            for tsub in range(4):
                p.dma("sync", lgr[:, tq * 4 + tsub, :],
                      lgacc[:, tsub * 8:(tsub + 1) * 8],
                      reads=[r_lgacc], writes=[dr["logits1"]], label=f"stlg{tq}.{tsub}")
            p.dma("sync", xnr[:, :, tq * 512:(tq + 1) * 512],
                  sxn3[:, :, tc * 512:(tc + 1) * 512],
                  reads=[r_xn], writes=[dr["xn1"]], label=f"stxn{tq}")

        for sect in range(2):
            wdram = d["wkT"] if sect == 0 else d["wvT"]
            wrr = wdram.ap().rearrange("(c p) j -> p c j", p=128)
            out_d = d["sk1"] if sect == 0 else d["v1"]
            for g in range(4):
                k = (h * 2 + sect) * 4 + g
                wb = k % 2
                for piece in range(4):
                    p.dma("gpsimd", swg3[wb][:, piece * 4:(piece + 1) * 4, :],
                          wrr[:, piece * 4:(piece + 1) * 4, g * 512:(g + 1) * 512],
                          writes=[r_wg[wb]], label=f"ldw{k}")
                for i in range(16):
                    pi = i % 2
                    ei = i % 4
                    if sect == 0:
                        jj, t = i // NTH, i % NTH
                        for c in range(NC16):
                            p.op("tensor", lambda e, wb=wb, c=c, jj=jj, t=t, pi=pi: e.matmul(
                                ps_mm[pi][:],
                                swg3[wb][:, c, jj * 128:(jj + 1) * 128],
                                sxn3[:, c, t * 512:(t + 1) * 512],
                                start=(c == 0), stop=(c == NC16 - 1)),
                                reads=[r_wg[wb], r_xn], writes=[r_psmm[pi]],
                                label=f"kmm{k}.{i}")
                        p.op("vector", lambda e, pi=pi: e.tensor_scalar(
                            ea[:], ps_mm[pi][:], 0.0, None, ALU.min),
                            reads=[r_psmm[pi]], writes=[r_ea], label=f"emin{k}.{i}")
                        p.op("vector", lambda e, pi=pi: e.tensor_scalar(
                            eb[:], ps_mm[pi][:], 0.0, None, ALU.max),
                            reads=[r_psmm[pi]], writes=[r_eb], label=f"emax{k}.{i}")
                        p.op("scalar", lambda e: e.activation(ea[:], ea[:], AF.Exp),
                             reads=[r_ea], writes=[r_ea], label=f"eexp{k}.{i}")
                        p.op("vector", lambda e, ei=ei: e.tensor_tensor(
                            sev[ei][:], ea[:], eb[:], ALU.add),
                            reads=[r_ea, r_eb], writes=[r_ev[ei]], label=f"eadd{k}.{i}")
                        dst = out_d.ap()[g * 512 + jj * 128:g * 512 + (jj + 1) * 128,
                                         h * TH + t * 512:h * TH + (t + 1) * 512]
                    else:
                        ts = i
                        for c in range(NC16):
                            p.op("tensor", lambda e, wb=wb, c=c, ts=ts, pi=pi: e.matmul(
                                ps_mm[pi][:],
                                sxn3[:, c, ts * 128:(ts + 1) * 128],
                                swg3[wb][:, c, :],
                                start=(c == 0), stop=(c == NC16 - 1)),
                                reads=[r_wg[wb], r_xn], writes=[r_psmm[pi]],
                                label=f"vmm{k}.{i}")
                        p.op("vector", lambda e, pi=pi, ei=ei: e.tensor_copy(
                            sev[ei][:], ps_mm[pi][:]),
                            reads=[r_psmm[pi]], writes=[r_ev[ei]], label=f"vev{k}.{i}")
                        dst = out_d.ap()[h * TH + ts * 128:h * TH + (ts + 1) * 128,
                                         g * 512:(g + 1) * 512]
                    p.dma("sync", dst, sev[ei][:], reads=[r_ev[ei]],
                          writes=[dr["sk1"] if sect == 0 else dr["v1"]],
                          label=f"st{k}.{i}")
    p.dma("sync", d["sumxn"].ap()[:, :], sxn1[:], reads=[r_sxn1],
          writes=[dr["sumxn"]], label="stsxn")


def build_p5(kind_inter="ExternalOutput"):
    nc = bass.Bass("TRN2", target_bir_lowering=False)
    d = {}
    d["x1"] = nc.dram_tensor("x1", [D, T], F32, kind="ExternalInput")
    d["wkT"] = nc.dram_tensor("wkT", [D, D], F32, kind="ExternalInput")
    d["wvT"] = nc.dram_tensor("wvT", [D, D], F32, kind="ExternalInput")
    d["gatewT"] = nc.dram_tensor("gatewT", [D, NE], F32, kind="ExternalInput")
    d["ones_col_bf"] = nc.dram_tensor("ones_col_bf", [128, 1], BF16, kind="ExternalInput")
    d["ones_row_f"] = nc.dram_tensor("ones_row_f", [1, 128], F32, kind="ExternalInput")
    d["xn1"] = nc.dram_tensor("xn1", [D, T], BF16, kind=kind_inter)
    d["sk1"] = nc.dram_tensor("sk1", [D, T], BF16, kind=kind_inter)
    d["v1"] = nc.dram_tensor("v1", [T, D], BF16, kind=kind_inter)
    d["logits1"] = nc.dram_tensor("logits1", [T, NE], F32, kind=kind_inter)
    d["sumxn"] = nc.dram_tensor("sumxn", [128, NC16], F32, kind=kind_inter)
    dr = {k: Res("dram_" + k) for k in d}
    p = Prog(nc)
    sc = Scope(nc)
    emit_p5(p, d, dr, sc)
    sc.close()
    p.finalize()
    return nc



def emit_p6(p, d, dr, sc):
    nc = p.nc
    NCH = TH // CHK

    maskt = sc.sb("p6_maskt", [128, 128], F32)
    identb = sc.sb("p6_identb", [128, 128], BF16)
    identf = sc.sb("p6_identf", [128, 128], F32)
    onesf = sc.sb("p6_onesf", [1, 128], F32)
    onesb = sc.sb("p6_onesb", [128, 1], BF16)
    onescf = sc.sb("p6_onescf", [128, 1], F32)
    soffv = sc.sb("p6_soffv", [1, 64], I32)
    sflag = sc.sb("p6_sflag", [1, 1], F32)
    growv = sc.sb("p6_grow", [1, NH], F32)
    grep = sc.sb("p6_grep", [128, NH], F32)
    gcomp = sc.sb("p6_gcomp", [128, NH], F32)
    flagrep = sc.sb("p6_flagrep", [128, 1], F32)
    invflag = sc.sb("p6_invflag", [128, 1], F32)
    kct = sc.sb("p6_kct", [16, 4], F32)
    lmb = sc.sb("p6_lmb", [128, 4 * NH], F32)
    memb = sc.sb("p6_memb", [128, 4 * NH * 129], BF16)
    sxnq = sc.sb("p6_sxnq", [128, NC16], F32)
    sxnb = sc.sb("p6_sxnb", [128, NC16], BF16)
    qs = sc.sb("p6_qs", [128, NH], F32)
    relb = sc.sb("p6_relb", [16, 4], F32)
    prodb = sc.sb("p6_prodb", [128, NH], F32)
    actrow = sc.sb("p6_actrow", [1, 4], F32)
    actrep = sc.sb("p6_actrep", [16, 4], F32)
    wsm = sc.sb("p6_wsm", [16, 4], F32)
    wcols = sc.sb("p6_wcols", [16, 1], F32)
    wrow = sc.sb("p6_wrow", [1, 16], F32)
    wrep = sc.sb("p6_wrep", [128, 4 * NH], F32)
    sq1own = sc.sb("p6_sq1own", [128, NC16 * TH], BF16)

    scps = Scope(nc)
    ps_qs = scps.ps("p6_psqs", [128, NH], F32)
    ps_small = scps.ps("p6_pssmall", [128, 16], F32)

    r_const = Res("const")
    r_offv = Res("offv")
    r_gr = Res("gr")
    r_flag = Res("flagrep")
    r_qs = Res("qs")
    r_sxnq = Res("sxnq")
    r_rel = Res("rel")
    r_prod = Res("prod")
    r_act = Res("act")
    r_w = Res("w")
    r_wrep = Res("wrep")
    r_xn1o = Res("xn1o")
    r_swq = [Res(f"swq{i}") for i in range(2)]
    r_sq1 = Res("sq1own")
    r_ea, r_eb = Res("ea"), Res("eb")
    r_psq = [Res(f"psq{i}") for i in range(2)]
    r_psqs = Res("psqs")
    r_pss = Res("pss")
    r_szr = [Res(f"szr{h}") for h in range(NH)]
    r_szb = [Res(f"szb{i}") for i in range(2)]
    r_skc = [Res(f"skc{i}") for i in range(2)]
    r_vc = [Res(f"vc{i}") for i in range(2)]
    r_vta = Res("vta")
    r_vtb = Res("vtb")
    r_sktt = Res("sktt")
    r_sTb = Res("sTb")
    r_macc = [Res(f"macc{i}") for i in range(2)]
    r_loct = Res("loct")
    r_o1t = Res("o1t")
    r_o1f = Res("o1f")
    r_cols = Res("colsc")
    r_aev = [Res(f"aev{i}") for i in range(2)]

    xnr = d["xn1"].ap().rearrange("(c p) t -> p c t", p=128)
    skr3 = d["sk1"].ap().rearrange("(c p) t -> p c t", p=128)
    v1ap = d["v1"].ap()
    wqr = d["wqT"].ap().rearrange("(c p) j -> p c j", p=128)
    wor = d["woT"].ap().rearrange("(c p) j -> p c j", p=128)
    memr = d["memcat"].ap()
    a1r = d["attn1"].ap().rearrange("(c p) t -> p c t", p=128)
    sq3 = sq1own[:].rearrange("p (c t) -> p c t", c=NC16)
    memb4 = memb[:].rearrange("p (m h z) -> p m h z", m=4, h=NH)

    def dynv(eng, qn, idx):
        return eng.value_load(soffv[0:1, idx:idx + 1])

    p.dma("sync", maskt[:], d["maskT1"][:, :], writes=[r_const])
    p.dma("gpsimd", identb[:], d["ident_bf"][:, :], writes=[r_const])
    p.dma("sync", identf[:], d["ident_f"][:, :], writes=[r_const])
    p.dma("sync", onesf[:], d["ones_row_f"][:, :], writes=[r_const])
    p.dma("gpsimd", onesb[:], d["ones_col_bf"][:, :], writes=[r_const])
    p.dma("sync", onescf[:], d["ones_col_f"][:, :], writes=[r_const])
    p.dma("sync", soffv[:], d["offv"][:, :], writes=[r_offv])
    p.dma("sync", sflag[:], d["flag"][:, :], writes=[r_const])
    p.dma("sync", kct[:], d["kcT"][:, :], writes=[r_const])
    for m in range(4):
        p.dma("sync", lmb[:, m * NH:(m + 1) * NH], d["lmT"].ap()[m, :, :],
              writes=[r_const], label=f"ldlm{m}")
        p.dma("gpsimd", memb4[:, m, :, :],
              memr[m, :, :, :].rearrange("h d z -> d h z"),
              writes=[r_const], label=f"ldmem{m}")
    p.dma("sync", growv[:], d["gate_row"][:, :], writes=[r_const])
    p.dma("sync", sxnq[:], d["sumxn"][:, :], reads=[dr["sumxn"]], writes=[r_sxnq],
          label="ldsxn")

    p.op("scalar", lambda e: e.activation(growv[:], growv[:], AF.Sigmoid),
         reads=[r_const], writes=[r_gr], deps=[], label="gsig")
    p.op("tensor", lambda e: e.matmul(ps_small[:], onesf[:, :], growv[0:1, :],
                                      start=True, stop=True),
         reads=[r_gr, r_const], writes=[r_pss], label="grepmm")
    p.op("vector", lambda e: e.tensor_copy(grep[:], ps_small[:]),
         reads=[r_pss], writes=[r_gr], label="grepev")
    p.op("vector", lambda e: e.tensor_scalar(gcomp[:], grep[:], -1.0, 1.0,
                                             ALU.mult, ALU.add),
         reads=[r_gr], writes=[r_gr], label="gcomp")
    p.op("tensor", lambda e: e.matmul(ps_small[:, 0:1], onesf[:, :], sflag[0:1, :],
                                      start=True, stop=True),
         reads=[r_const], writes=[r_pss], label="flagmm")
    p.op("vector", lambda e: e.tensor_copy(flagrep[:], ps_small[:, 0:1]),
         reads=[r_pss], writes=[r_flag], label="flagev")
    p.op("vector", lambda e: e.tensor_scalar(invflag[:], flagrep[:], -1.0, 1.0,
                                             ALU.mult, ALU.add),
         reads=[r_flag], writes=[r_flag], label="invflag")

    scq = Scope(nc)
    ea = scq.sb("p6_ea", [128, 512], F32)
    eb = scq.sb("p6_eb", [128, 512], F32)
    swq = [scq.sb(f"p6_swq{i}", [128, NC16 * 512], BF16) for i in range(2)]
    sxn1o = scq.sb("p6_sxn1o", [128, NC16 * TH], BF16)
    bla = scq.sb("p6_bla", [128, TH], BF16)
    blb = scq.sb("p6_blb", [128, TH], BF16)
    r_bla, r_blb = Res("bla"), Res("blb")
    ps_q = [scq.ps(f"p6_psq{i}", [128, 512], F32) for i in range(2)]
    sxo3 = sxn1o[:].rearrange("p (c t) -> p c t", c=NC16)
    swq3 = [b[:].rearrange("p (c j) -> p c j", c=NC16) for b in swq]

    for c in range(NC16):
        p.dma("gpsimd", bla[:], d["xn1"].ap()[c * 128:(c + 1) * 128, 0:TH],
              reads=[dr["xn1"]], writes=[r_bla], label=f"ldxna{c}")
        p.dma("gpsimd", blb[:], d["xn1"].ap()[c * 128:(c + 1) * 128, TH:T],
              reads=[dr["xn1"]], writes=[r_blb], label=f"ldxnb{c}")
        p.op("vector", lambda e, c=c: e.tensor_scalar(
            sxo3[:, c, :], blb[:], flagrep[:, 0:1], None, ALU.mult),
            reads=[r_blb, r_flag], writes=[r_xn1o], label=f"xnd{c}")
        p.op("vector", lambda e: e.tensor_scalar(
            blb[:], bla[:], invflag[:, 0:1], None, ALU.mult),
            reads=[r_bla, r_flag], writes=[r_blb], label=f"xnf{c}")
        p.op("vector", lambda e, c=c: e.tensor_tensor(
            sxo3[:, c, :], sxo3[:, c, :], blb[:], ALU.add),
            reads=[r_blb, r_xn1o], writes=[r_xn1o], label=f"xns{c}")
    p.op("vector", lambda e: e.tensor_copy(sxnb[:], sxnq[:]),
         reads=[r_sxnq], writes=[r_sxnq], label="sxncast")

    NTH4 = TH // 512
    for g in range(4):
        wb = g % 2
        for piece in range(4):
            p.dma("gpsimd", swq3[wb][:, piece * 4:(piece + 1) * 4, :],
                  wqr[:, piece * 4:(piece + 1) * 4, g * 512:(g + 1) * 512],
                  writes=[r_swq[wb]], label=f"ldwq{g}")
        for i in range(16):
            jj, t = i // NTH4, i % NTH4
            pi = i % 2
            for c in range(NC16):
                p.op("tensor", lambda e, wb=wb, c=c, jj=jj, t=t, pi=pi: e.matmul(
                    ps_q[pi][:], swq3[wb][:, c, jj * 128:(jj + 1) * 128],
                    sxo3[:, c, t * 512:(t + 1) * 512],
                    start=(c == 0), stop=(c == NC16 - 1)),
                    reads=[r_swq[wb], r_xn1o], writes=[r_psq[pi]],
                    label=f"qmm{g}.{i}")
            p.op("vector", lambda e, pi=pi: e.tensor_scalar(
                ea[:], ps_q[pi][:], 0.0, None, ALU.min),
                reads=[r_psq[pi]], writes=[r_ea], label=f"qemin{g}.{i}")
            p.op("vector", lambda e, pi=pi: e.tensor_scalar(
                eb[:], ps_q[pi][:], 0.0, None, ALU.max),
                reads=[r_psq[pi]], writes=[r_eb], label=f"qemax{g}.{i}")
            p.op("scalar", lambda e: e.activation(ea[:], ea[:], AF.Exp),
                 reads=[r_ea], writes=[r_ea], label=f"qexp{g}.{i}")
            p.op("vector", lambda e, g=g, jj=jj, t=t: e.tensor_tensor(
                sq3[:, g * 4 + jj, t * 512:(t + 1) * 512], ea[:], eb[:], ALU.add),
                reads=[r_ea, r_eb], writes=[r_sq1], label=f"qeadd{g}.{i}")
        for jj in range(4):
            jc = g * 4 + jj
            for c in range(NC16):
                p.op("tensor", lambda e, wb=wb, c=c, jj=jj, jc=jc: e.matmul(
                    ps_qs[:, jc:jc + 1], swq3[wb][:, c, jj * 128:(jj + 1) * 128],
                    sxnb[:, c:c + 1], start=(c == 0), stop=(c == NC16 - 1),
                    skip_group_check=True),
                    reads=[r_swq[wb], r_sxnq], writes=[r_psqs], label=f"qsmm{jc}.{c}")
            p.op("vector", lambda e, jc=jc: e.tensor_copy(
                qs[:, jc:jc + 1], ps_qs[:, jc:jc + 1]),
                reads=[r_psqs], writes=[r_qs], label=f"qsev{jc}")
    scq.close()
    p.barrier()

    woT = sc.sb("p6_woT", [128, NC16 * D], BF16)
    szr = sc.sb("p6_szr", [128, NH * 129], F32)
    szb = [sc.sb(f"p6_szb{i}", [128, 129], BF16) for i in range(2)]
    skc = [sc.sb(f"p6_skc{i}", [128, NH * 128], BF16) for i in range(2)]
    vc = [sc.sb(f"p6_vc{i}", [128, NH * 128], BF16) for i in range(2)]
    vta = sc.sb("p6_vta", [128, NH * 128], BF16)
    vtb = sc.sb("p6_vtb", [128, NH * 128], BF16)
    sktt = sc.sb("p6_sktt", [128, 128], BF16)
    sTb = sc.sb("p6_sTb", [128, 128], BF16)
    macc = [sc.sb(f"p6_macc{i}", [128, 128], F32) for i in range(2)]
    loct = sc.sb("p6_loct", [128, 128], F32)
    o1t = sc.sb("p6_o1t", [128, 128], BF16)
    o1f = sc.sb("p6_o1f", [128, NC16 * 512], BF16)
    colsc = sc.sb("p6_colsc", [128, 8], F32)
    aev = [sc.sb(f"p6_aev{i}", [128, 512], F32) for i in range(2)]
    szr3 = szr[:].rearrange("p (h z) -> p h z", h=NH)
    skc3 = [b[:].rearrange("p (h s) -> p h s", h=NH) for b in skc]
    vc3 = [b[:].rearrange("p (h s) -> p h s", h=NH) for b in vc]
    wo3 = woT[:].rearrange("p (c j) -> p c j", c=NC16)
    o1f3 = o1f[:].rearrange("p (c t) -> p c t", c=NC16)
    for piece in range(4):
        p.dma("gpsimd", wo3[:, piece * 4:(piece + 1) * 4, :],
              wor[:, piece * 4:(piece + 1) * 4, :], writes=[r_const],
              label=f"ldwo{piece}")

    for m in range(4):
        p.op("vector", lambda e, m=m: e.tensor_tensor(
            prodb[:], qs[:], lmb[:, m * NH:(m + 1) * NH], ALU.mult),
            reads=[r_qs, r_const], writes=[r_prod], label=f"relp{m}")
        p.op("tensor", lambda e: e.matmul(ps_small[0:16, 0:1], prodb[:], onescf[:, 0:1],
                                          start=True, stop=True),
             reads=[r_prod, r_const], writes=[r_pss], label=f"relmm{m}")
        p.op("vector", lambda e, m=m: e.tensor_scalar(
            relb[:, m:m + 1], ps_small[0:16, 0:1], 1.0 / T, None, ALU.mult),
            reads=[r_pss], writes=[r_rel], label=f"relev{m}")
    p.op("tensor", lambda e: e.matmul(ps_small[0:1, 0:4], onescf[0:16, 0:1], kct[:, :],
                                      start=True, stop=True),
         reads=[r_const], writes=[r_pss], label="kcmm")
    p.op("vector", lambda e: e.tensor_scalar(actrow[:], ps_small[0:1, 0:4], EPS0, None,
                                             ALU.is_ge),
         reads=[r_pss], writes=[r_act], label="actrow")
    p.op("tensor", lambda e: e.matmul(ps_small[0:16, 0:4], onesf[0:1, 0:16], actrow[:, :],
                                      start=True, stop=True),
         reads=[r_act, r_const], writes=[r_pss], label="actrepmm")
    p.op("vector", lambda e: e.tensor_scalar(actrep[:], ps_small[0:16, 0:4], 1e30, -1e30,
                                             ALU.mult, ALU.add),
         reads=[r_pss], writes=[r_act], label="actrep")
    p.op("vector", lambda e: e.tensor_tensor(relb[:], relb[:], actrep[:], ALU.add),
         reads=[r_act, r_rel], writes=[r_rel], label="relmask")
    p.op("vector", lambda e: e.reduce_max(wcols[:, 0:1], relb[:], axis=AX.X),
         reads=[r_rel], writes=[r_w], label="wmax")
    p.op("vector", lambda e: e.tensor_scalar(wsm[:], relb[:], wcols[0:16, 0:1], None,
                                             ALU.subtract),
         reads=[r_rel, r_w], writes=[r_w], label="wsub")
    p.op("scalar", lambda e: e.activation(wsm[:], wsm[:], AF.Exp),
         reads=[r_w], writes=[r_w], label="wexp")
    p.op("vector", lambda e: e.reduce_sum(wcols[:, 0:1], wsm[:], axis=AX.X),
         reads=[r_w], writes=[r_w], label="wsumr")
    p.op("vector", lambda e: e.reciprocal(wcols[:, 0:1], wcols[0:16, 0:1]),
         reads=[r_w], writes=[r_w], label="wrec")
    p.op("vector", lambda e: e.tensor_scalar(wsm[:], wsm[:], wcols[0:16, 0:1], None,
                                             ALU.mult),
         reads=[r_w], writes=[r_w], label="wnorm")
    for m in range(4):
        p.op("vector", lambda e, m=m: e.tensor_copy(wcols[:, 0:1], wsm[:, m:m + 1]),
             reads=[r_w], writes=[r_w], label=f"wcol{m}")
        p.op("tensor", lambda e: e.transpose(ps_small[0:1, 0:16], wcols[0:16, 0:1],
                                             identf[0:16, 0:16]),
             reads=[r_w, r_const], writes=[r_pss], label=f"wtr{m}")
        p.op("vector", lambda e: e.tensor_copy(wrow[:], ps_small[0:1, 0:16]),
             reads=[r_pss], writes=[r_w], label=f"wrowev{m}")
        p.op("tensor", lambda e: e.matmul(ps_small[:], onesf[:, :], wrow[:, :],
                                          start=True, stop=True),
             reads=[r_w, r_const], writes=[r_pss], label=f"wrepmm{m}")
        p.op("vector", lambda e, m=m: e.tensor_copy(wrep[:, m * NH:(m + 1) * NH],
                                                    ps_small[:]),
             reads=[r_pss], writes=[r_wrep], label=f"wrepev{m}")

    scps.close()
    p.barrier()
    ps_S = sc.ps("p6_psS", [128, 129], F32)
    ps_tr = sc.ps("p6_pstr", [128, 128], BF16)
    r_psS = Res("psS")
    r_pstr = Res("pstr")
    for h in range(NH):
        hb = h % 2
        p.dma("sync", skc[hb][:], d["sk1"].ap()[h * 128:(h + 1) * 128, 0:TH],
              reads=[dr["sk1"]], writes=[r_skc[hb]], label=f"ldskh{h}")
        p.dma("sync", vc3[hb][:, :, :],
              d["v1"].ap()[0:TH, h * 128:(h + 1) * 128].rearrange("(n p) j -> p n j", p=128),
              reads=[dr["v1"]], writes=[r_vc[hb]], label=f"ldvh{h}")
        for cc in range(NCH):
            p.op("tensor", lambda e, hb=hb, cc=cc: e.transpose(
                ps_tr[:], skc[hb][:, cc * 128:(cc + 1) * 128], identb[:]),
                reads=[r_skc[hb], r_const], writes=[r_pstr], label=f"pfx_tr{h}.{cc}")
            p.op("scalar", lambda e: e.copy(sktt[:], ps_tr[:]),
                 reads=[r_pstr], writes=[r_sktt], label=f"pfx_te{h}.{cc}")
            p.op("tensor", lambda e, hb=hb, cc=cc: e.matmul(
                ps_S[:, 0:128], sktt[:], vc3[hb][:, cc, :],
                start=(cc == 0), stop=False, skip_group_check=True),
                reads=[r_sktt, r_vc[hb]], writes=[r_psS], label=f"pfx_s{h}.{cc}")
            p.op("tensor", lambda e, cc=cc: e.matmul(
                ps_S[:, 128:129], sktt[:], onesb[:, 0:1],
                start=False, stop=(cc == NCH - 1), skip_group_check=True),
                reads=[r_sktt, r_const], writes=[r_psS], label=f"pfx_z{h}.{cc}")
        p.op("vector", lambda e, h=h: e.tensor_scalar(
            szr3[:, h, :], ps_S[:], flagrep[:, 0:1], None, ALU.mult),
            reads=[r_psS, r_flag], writes=[r_szr[h]], label=f"pfx_ev{h}")

    ps_sc2 = sc.ps("p6_pssc2", [128, 128], F32)
    ps_o = sc.ps("p6_pso", [128, 129], F32)
    ps_am = sc.ps("p6_psam", [128, 129], F32)
    ps_tr2 = sc.ps("p6_pstr2", [128, 128], BF16)
    ps_wo = [sc.ps(f"p6_pswo{i}", [128, 512], F32) for i in range(2)]
    r_psc2 = Res("psc2")
    r_pso = Res("pso")
    r_psam = Res("psam")
    r_pstr2 = Res("pstr2")
    r_pswo = [Res(f"pswo{i}") for i in range(2)]

    for cch in range(NCH):
        cb = cch % 2
        skr3b = d["sk1"].ap().rearrange("(c p) t -> p c t", p=128)
        p.dma("sync", vta[:], skr3b[:, :, cch * 128:(cch + 1) * 128],
              reads=[dr["sk1"]], writes=[r_vta], label=f"ldska{cch}")
        p.dma("sync", vtb[:], skr3b[:, :, TH + cch * 128:TH + (cch + 1) * 128],
              reads=[dr["sk1"]], writes=[r_vtb], label=f"ldskb{cch}")
        p.op("vector", lambda e, cb=cb: e.tensor_scalar(
            skc[cb][:], vtb[:], flagrep[:, 0:1], None, ALU.mult),
            reads=[r_vtb, r_flag], writes=[r_skc[cb]], label=f"skd{cch}")
        p.op("vector", lambda e: e.tensor_scalar(
            vtb[:], vta[:], invflag[:, 0:1], None, ALU.mult),
            reads=[r_vta, r_flag], writes=[r_vtb], label=f"skf{cch}")
        p.op("vector", lambda e, cb=cb: e.tensor_tensor(
            skc[cb][:], skc[cb][:], vtb[:], ALU.add),
            reads=[r_vtb, r_skc[cb]], writes=[r_skc[cb]], label=f"sks{cch}")
        p.dma("sync", vta[:], d["v1"].ap()[cch * 128:(cch + 1) * 128, 0:D],
              reads=[dr["v1"]], writes=[r_vta], label=f"ldva{cch}")
        p.dma("sync", vtb[:], d["v1"].ap()[TH + cch * 128:TH + (cch + 1) * 128, 0:D],
              reads=[dr["v1"]], writes=[r_vtb], label=f"ldvb{cch}")
        p.op("vector", lambda e, cb=cb: e.tensor_scalar(
            vc[cb][:], vtb[:], flagrep[:, 0:1], None, ALU.mult),
            reads=[r_vtb, r_flag], writes=[r_vc[cb]], label=f"vdif{cch}")
        p.op("vector", lambda e: e.tensor_scalar(
            vtb[:], vta[:], invflag[:, 0:1], None, ALU.mult),
            reads=[r_vta, r_flag], writes=[r_vtb], label=f"vfl{cch}")
        p.op("vector", lambda e, cb=cb: e.tensor_tensor(
            vc[cb][:], vc[cb][:], vtb[:], ALU.add),
            reads=[r_vtb, r_vc[cb]], writes=[r_vc[cb]], label=f"vsel{cch}")
        for h in range(NH):
            sb_ = h % 2
            p.op("scalar", lambda e, h=h, sb_=sb_: e.copy(szb[sb_][:], szr3[:, h, :]),
                 reads=[r_szr[h]], writes=[r_szb[sb_]], label=f"szc{cch}.{h}")
            p.op("tensor", lambda e, cb=cb, h=h, cch=cch: e.matmul(
                ps_sc2[:], skc3[cb][:, h, :], sq3[:, h, cch * 128:(cch + 1) * 128],
                start=True, stop=True),
                reads=[r_skc[cb], r_sq1], writes=[r_psc2], label=f"sc{cch}.{h}")
            p.op("vector", lambda e: e.tensor_tensor(sTb[:], ps_sc2[:], maskt[:],
                                                     ALU.mult),
                reads=[r_psc2, r_const], writes=[r_sTb], label=f"scm{cch}.{h}")
            p.op("tensor", lambda e, h=h, cch=cch, sb_=sb_: e.matmul(
                ps_o[:], sq3[:, h, cch * 128:(cch + 1) * 128], szb[sb_][:],
                start=True, stop=False, skip_group_check=True),
                reads=[r_sq1, r_szb[sb_]], writes=[r_pso], label=f"onv{cch}.{h}")
            p.op("tensor", lambda e, cb=cb, h=h: e.matmul(
                ps_o[:, 0:128], sTb[:], vc3[cb][:, h, :],
                start=False, stop=False, skip_group_check=True),
                reads=[r_sTb, r_vc[cb]], writes=[r_pso], label=f"odg{cch}.{h}")
            p.op("tensor", lambda e: e.matmul(
                ps_o[:, 128:129], sTb[:], onesb[:, 0:1],
                start=False, stop=True, skip_group_check=True),
                reads=[r_sTb, r_const], writes=[r_pso], label=f"osm{cch}.{h}")
            p.op("tensor", lambda e, cb=cb, h=h: e.transpose(
                ps_tr[:], skc3[cb][:, h, :], identb[:]),
                reads=[r_skc[cb], r_const], writes=[r_pstr], label=f"str{cch}.{h}")
            p.op("scalar", lambda e: e.copy(sktt[:], ps_tr[:]),
                 reads=[r_pstr], writes=[r_sktt], label=f"ste{cch}.{h}")
            p.op("tensor", lambda e, cb=cb, h=h: e.matmul(
                ps_S[:, 0:128], sktt[:], vc3[cb][:, h, :],
                start=True, stop=False, skip_group_check=True),
                reads=[r_sktt, r_vc[cb]], writes=[r_psS], label=f"sup{cch}.{h}")
            p.op("tensor", lambda e: e.matmul(
                ps_S[:, 128:129], sktt[:], onesb[:, 0:1],
                start=False, stop=True, skip_group_check=True),
                reads=[r_sktt, r_const], writes=[r_psS], label=f"szup{cch}.{h}")
            p.op("vector", lambda e, h=h: e.tensor_tensor(
                szr3[:, h, :], szr3[:, h, :], ps_S[:], ALU.add),
                reads=[r_psS, r_szr[h]], writes=[r_szr[h]], label=f"sza{cch}.{h}")
            for m in range(4):
                mb = m % 2
                p.op("tensor", lambda e, h=h, cch=cch, m=m: e.matmul(
                    ps_am[:], sq3[:, h, cch * 128:(cch + 1) * 128],
                    memb4[:, m, h, :], start=True, stop=True),
                    reads=[r_sq1, r_const], writes=[r_psam], label=f"am{cch}.{h}.{m}")
                p.op("vector", lambda e: e.tensor_scalar(
                    colsc[:, 0:1], ps_am[:, 128:129], EPS0, None, ALU.max),
                    reads=[r_psam], writes=[r_cols], label=f"amn{cch}.{h}.{m}")
                p.op("vector", lambda e: e.reciprocal(colsc[:, 1:2], colsc[:, 0:1]),
                     reads=[r_cols], writes=[r_cols])
                p.op("vector", lambda e, m=m, h=h: e.tensor_tensor(
                    colsc[:, 2:3], colsc[:, 1:2], wrep[:, m * NH + h:m * NH + h + 1],
                    ALU.mult),
                    reads=[r_cols, r_wrep], writes=[r_cols], label=f"amw{cch}.{h}.{m}")
                if m == 0:
                    p.op("vector", lambda e, mb=mb: e.tensor_scalar(
                        macc[mb][:], ps_am[:, 0:128], colsc[:, 2:3], None, ALU.mult),
                        reads=[r_psam, r_cols], writes=[r_macc[mb]],
                        label=f"ama{cch}.{h}.{m}")
                else:
                    p.op("vector", lambda e, mb=mb: e.scalar_tensor_tensor(
                        macc[mb][:], ps_am[:, 0:128], colsc[:, 2:3], macc[1 - mb][:],
                        ALU.mult, ALU.add),
                        reads=[r_psam, r_cols, r_macc[1 - mb]], writes=[r_macc[mb]],
                        label=f"ama{cch}.{h}.{m}")
            p.op("vector", lambda e: e.tensor_scalar(
                colsc[:, 3:4], ps_o[:, 128:129], EPS0, None, ALU.max),
                reads=[r_pso], writes=[r_cols], label=f"lon{cch}.{h}")
            p.op("vector", lambda e: e.reciprocal(colsc[:, 4:5], colsc[:, 3:4]),
                 reads=[r_cols], writes=[r_cols])
            p.op("vector", lambda e, h=h: e.tensor_tensor(
                colsc[:, 5:6], colsc[:, 4:5], gcomp[:, h:h + 1], ALU.mult),
                reads=[r_cols, r_gr], writes=[r_cols], label=f"lgc{cch}.{h}")
            p.op("vector", lambda e: e.tensor_scalar(
                loct[:], ps_o[:, 0:128], colsc[:, 5:6], None, ALU.mult),
                reads=[r_pso, r_cols], writes=[r_loct], label=f"lsc{cch}.{h}")
            p.op("vector", lambda e, h=h: e.scalar_tensor_tensor(
                o1t[:], macc[1][:], grep[:, h:h + 1], loct[:], ALU.mult, ALU.add),
                reads=[r_macc[1], r_gr, r_loct], writes=[r_o1t],
                label=f"cmb{cch}.{h}")
            p.op("tensor", lambda e: e.transpose(ps_tr2[:], o1t[:], identb[:]),
                 reads=[r_o1t, r_const], writes=[r_pstr2], label=f"otr{cch}.{h}")
            p.op("scalar", lambda e, h=h, cch=cch: e.copy(
                o1f3[:, h, (cch % 4) * 128:(cch % 4 + 1) * 128], ps_tr2[:]),
                reads=[r_pstr2], writes=[r_o1f], label=f"ote{cch}.{h}")
        if cch % 4 == 3:
            gidx = cch // 4
            for j in range(NC16):
                pw = j % 2
                for hc in range(NC16):
                    p.op("tensor", lambda e, j=j, hc=hc, pw=pw: e.matmul(
                        ps_wo[pw][:], wo3[:, hc, j * 128:(j + 1) * 128],
                        o1f3[:, hc, :], start=(hc == 0), stop=(hc == NC16 - 1)),
                        reads=[r_const, r_o1f], writes=[r_pswo[pw]],
                        label=f"wo{gidx}.{j}")
                p.op("vector", lambda e, pw=pw: e.tensor_copy(aev[pw][:], ps_wo[pw][:]),
                     reads=[r_pswo[pw]], writes=[r_aev[pw]], label=f"woev{gidx}.{j}")
                p.dma("sync", a1r[:, j, gidx * 512:(gidx + 1) * 512], aev[pw][:],
                      reads=[r_aev[pw]], writes=[dr["attn1"]], label=f"stwo{gidx}.{j}")


def build_p6(kind_inter="ExternalOutput"):
    nc = bass.Bass("TRN2", target_bir_lowering=False)
    d = {}
    d["xn1"] = nc.dram_tensor("xn1", [D, T], BF16, kind="ExternalInput")
    d["sk1"] = nc.dram_tensor("sk1", [D, T], BF16, kind="ExternalInput")
    d["v1"] = nc.dram_tensor("v1", [T, D], BF16, kind="ExternalInput")
    d["sumxn"] = nc.dram_tensor("sumxn", [128, NC16], F32, kind="ExternalInput")
    d["wqT"] = nc.dram_tensor("wqT", [D, D], F32, kind="ExternalInput")
    d["woT"] = nc.dram_tensor("woT", [D, D], F32, kind="ExternalInput")
    d["memcat"] = nc.dram_tensor("memcat", [4, NH, HD, 129], F32, kind="ExternalInput")
    d["lmT"] = nc.dram_tensor("lmT", [4, HD, NH], F32, kind="ExternalInput")
    d["kcT"] = nc.dram_tensor("kcT", [NH, 4], F32, kind="ExternalInput")
    d["gate_row"] = nc.dram_tensor("gate_row", [1, NH], F32, kind="ExternalInput")
    d["maskT1"] = nc.dram_tensor("maskT1", [CHK, CHK], F32, kind="ExternalInput")
    d["ident_bf"] = nc.dram_tensor("ident_bf", [128, 128], BF16, kind="ExternalInput")
    d["ident_f"] = nc.dram_tensor("ident_f", [128, 128], F32, kind="ExternalInput")
    d["ones_row_f"] = nc.dram_tensor("ones_row_f", [1, 128], F32, kind="ExternalInput")
    d["ones_col_bf"] = nc.dram_tensor("ones_col_bf", [128, 1], BF16, kind="ExternalInput")
    d["ones_col_f"] = nc.dram_tensor("ones_col_f", [128, 1], F32, kind="ExternalInput")
    d["offv"] = nc.dram_tensor("offv", [1, 64], I32, kind="ExternalInput")
    d["flag"] = nc.dram_tensor("flag", [1, 1], F32, kind="ExternalInput")
    d["attn1"] = nc.dram_tensor("attn1", [D, TH], F32, kind=kind_inter)
    dr = {k: Res("dram_" + k) for k in d}
    p = Prog(nc)
    sc = Scope(nc)
    emit_p6(p, d, dr, sc)
    sc.close()
    p.finalize()
    return nc



def emit_p7(p, d, dr, sc):
    nc = p.nc
    NHE = HE // 128
    NHS = HS // 128

    onesf = sc.sb("p7_onesf", [1, 128], F32)
    identb = sc.sb("p7_identb", [128, 128], BF16)
    soffv = sc.sb("p7_soffv", [1, 64], I32)
    eselb = sc.sb("p7_eselb", [8, 8 * 128], BF16)
    shgc = sc.sb("p7_shgc", [128, NC16], F32)
    shgb = sc.sb("p7_shgb", [128, NC16], BF16)
    combf = sc.sb("p7_combf", [8, TH], BF16)
    srow = sc.sb("p7_srow", [1, 1024], F32)
    sxn = sc.sb("p7_sxn", [128, NC16 * 1024], BF16)
    acc = sc.sb("p7_acc", [128, NC16 * 1024], F32)
    hbuf = sc.sb("p7_hbuf", [128, NHE * 1024], BF16)
    wge = [sc.sb(f"p7_wge{i}", [128, NC16 * 128], BF16) for i in range(2)]
    wue = [sc.sb(f"p7_wue{i}", [128, NC16 * 128], BF16) for i in range(2)]
    wde = sc.sb("p7_wde", [128, NHS * 512], BF16)
    combr = sc.sb("p7_combr", [128, 1024], BF16)
    sgr = sc.sb("p7_sgr", [128, 1024], BF16)
    sgv = sc.sb("p7_sgv", [128, 512], BF16)
    suv = sc.sb("p7_suv", [128, 512], BF16)
    h1v = sc.sb("p7_h1v", [128, 512], BF16)
    att = sc.sb("p7_att", [128, 512], F32)
    x1a = sc.sb("p7_x1a", [128, 512], F32)
    x1b = sc.sb("p7_x1b", [128, 512], F32)
    xba = sc.sb("p7_xba", [128, 1024], BF16)
    xbb = sc.sb("p7_xbb", [128, 1024], BF16)
    outt = [sc.sb(f"p7_outt{i}", [128, 512], F32) for i in range(2)]


    r_const = Res("const")
    r_offv = Res("offv")
    r_combf = Res("combf")
    r_srow = Res("srow")
    r_xn = Res("xn")
    r_acc = [Res(f"acc{j}") for j in range(NC16 * 2)]
    r_hb = [Res(f"hb{c}") for c in range(NHS)]
    r_wge = [Res(f"wge{i}") for i in range(2)]
    r_wue = [Res(f"wue{i}") for i in range(2)]
    r_wde = Res("wde")
    r_combr = Res("combr")
    r_sgr = Res("sgr")
    r_sgv, r_suv, r_h1v = Res("sgv"), Res("suv"), Res("h1v")
    r_att, r_x1t = Res("att"), Res("x1t")
    r_x1a, r_x1b = Res("x1a"), Res("x1b")
    r_xba, r_xbb = Res("xba"), Res("xbb")
    r_outt = [Res(f"outt{i}") for i in range(2)]
    r_psg = [Res(f"psg{i}") for i in range(2)]
    r_psu = [Res(f"psu{i}") for i in range(2)]
    r_psd = [Res(f"psd{i}") for i in range(2)]
    r_pssg = Res("pssg")
    r_psrep = Res("psrep")

    lgap = d["logits1"].ap()
    outr = d["outh"].ap().rearrange("(c p) t -> p c t", p=128)
    a1r = d["attn1"].ap().rearrange("(c p) t -> p c t", p=128)
    sxn3 = sxn[:].rearrange("p (c t) -> p c t", c=NC16)
    acc3 = acc[:].rearrange("p (c t) -> p c t", c=NC16)
    hb3 = hbuf[:].rearrange("p (c t) -> p c t", c=NHE)
    hb3s = hbuf[:].rearrange("p (c t) -> p c t", c=NHS)
    wge3 = [b[:].rearrange("p (c j) -> p c j", c=NC16) for b in wge]
    wue3 = [b[:].rearrange("p (c j) -> p c j", c=NC16) for b in wue]
    wde3 = wde[:].rearrange("p (c j) -> p c j", c=NHS)

    def dynv(e, qn, idx):
        return e.value_load(soffv[0:1, idx:idx + 1])

    p.dma("sync", onesf[:], d["ones_row_f"][:, :], writes=[r_const])
    p.dma("sync", soffv[:], d["offv"][:, :], writes=[r_offv])
    p.dma("gpsimd", eselb[:], d["esel_bf"][:, :], writes=[r_const])
    p.dma("gpsimd", identb[:], d["ident_bf"][:, :], writes=[r_const])
    p.dma("sync", shgc[:],
          d["shgate_col"].ap().rearrange("(c p) x -> p (c x)", p=128),
          writes=[r_const], label="ldshg", allow_slow_non_contiguous=True)
    p.op("vector", lambda e: e.tensor_copy(shgb[:], shgc[:]),
         reads=[r_const], writes=[r_const], label="shgcast")

    flagrep7 = sc.sb("p7_flagrep", [128, 1], F32)
    invflag7 = sc.sb("p7_invflag", [128, 1], F32)
    sflag7 = sc.sb("p7_sflag", [1, 1], F32)
    scp = Scope(nc)
    lgt = scp.sb("p7_lgt", [128, 8], F32)
    lga = scp.sb("p7_lga", [128, 8], F32)
    lgb = scp.sb("p7_lgb", [128, 8], F32)
    ps_fl = scp.ps("p7_psfl", [128, 1], F32)
    r_lga = Res("lga")
    r_lgb = Res("lgb")
    r_fl7 = Res("fl7")
    r_psfl = Res("psfl")
    p.dma("sync", sflag7[:], d["flag"][:, :], writes=[r_fl7])
    p.op("tensor", lambda e: e.matmul(ps_fl[:], onesf[:, :], sflag7[0:1, :],
                                      start=True, stop=True),
         reads=[r_fl7, r_const], writes=[r_psfl], label="flagmm7")
    p.op("vector", lambda e: e.tensor_copy(flagrep7[:], ps_fl[:]),
         reads=[r_psfl], writes=[r_fl7], label="flagev7")
    p.op("vector", lambda e: e.tensor_scalar(invflag7[:], flagrep7[:], -1.0, 1.0,
                                             ALU.mult, ALU.add),
         reads=[r_fl7], writes=[r_fl7], label="invflag7")
    prb = scp.sb("p7_prb", [128, 8], F32)
    pco = scp.sb("p7_pco", [128, 8], F32)
    cbb = scp.sb("p7_cbb", [128, 8], BF16)
    ccol = scp.sb("p7_ccol", [128, 6], F32)
    ps_ctr = scp.ps("p7_psctr", [8, 128], BF16)
    r_lg = Res("lgt")
    r_prb = Res("prb")
    r_cc = Res("ccol")
    r_cb = Res("cbb")
    r_pctr = Res("pctr")

    for n in range(16):
        p.dma("sync", lga[:], lgap[n * 128:(n + 1) * 128, 0:NE],
              reads=[dr["logits1"]], writes=[r_lga], label=f"ldlga{n}")
        p.dma("sync", lgb[:], lgap[TH + n * 128:TH + (n + 1) * 128, 0:NE],
              reads=[dr["logits1"]], writes=[r_lgb], label=f"ldlgb{n}")
        p.op("vector", lambda e: e.tensor_tensor(lgt[:], lgb[:], lga[:], ALU.subtract),
             reads=[r_lga, r_lgb], writes=[r_lg], label=f"lgdif{n}")
        p.op("vector", lambda e: e.tensor_scalar(lgb[:], lgt[:], flagrep7[:, 0:1],
                                                 None, ALU.mult),
             reads=[r_lg, r_fl7], writes=[r_lgb], label=f"lgfl{n}")
        p.op("vector", lambda e: e.tensor_tensor(lgt[:], lgb[:], lga[:], ALU.add),
             reads=[r_lga, r_lgb], writes=[r_lg], label=f"lgsel{n}")
        p.op("vector", lambda e: e.reduce_max(ccol[:, 0:1], lgt[:], axis=AX.X),
             reads=[r_lg], writes=[r_cc], label=f"cmax{n}")
        p.op("vector", lambda e: e.tensor_scalar(prb[:], lgt[:], ccol[:, 0:1], None,
                                                 ALU.subtract),
             reads=[r_lg, r_cc], writes=[r_prb], label=f"csub{n}")
        p.op("scalar", lambda e: e.activation(prb[:], prb[:], AF.Exp),
             reads=[r_prb], writes=[r_prb], label=f"cexp{n}")
        p.op("vector", lambda e: e.reduce_sum(ccol[:, 1:2], prb[:], axis=AX.X),
             reads=[r_prb], writes=[r_cc], label=f"csum{n}")
        p.op("vector", lambda e: e.reciprocal(ccol[:, 2:3], ccol[:, 1:2]),
             reads=[r_cc], writes=[r_cc])
        p.op("vector", lambda e: e.tensor_scalar(prb[:], prb[:], ccol[:, 2:3], None,
                                                 ALU.mult),
             reads=[r_cc, r_prb], writes=[r_prb], label=f"cnrm{n}")
        p.op("vector", lambda e: e.reduce_max(ccol[:, 0:1], prb[:], axis=AX.X),
             reads=[r_prb], writes=[r_cc], label=f"cm1{n}")
        p.op("vector", lambda e: e.tensor_scalar(pco[:], prb[:], ccol[:, 0:1], None,
                                                 ALU.is_equal),
             reads=[r_prb, r_cc], writes=[r_cb], label=f"ceq{n}")
        p.op("vector", lambda e: e.tensor_scalar(pco[:], pco[:], -2.0, None, ALU.mult),
             reads=[r_cb], writes=[r_cb])
        p.op("vector", lambda e: e.tensor_tensor(pco[:], prb[:], pco[:], ALU.add),
             reads=[r_prb, r_cb], writes=[r_cb])
        p.op("vector", lambda e: e.reduce_max(ccol[:, 3:4], pco[:], axis=AX.X),
             reads=[r_cb], writes=[r_cc], label=f"cm2{n}")
        p.op("vector", lambda e: e.tensor_tensor(ccol[:, 4:5], ccol[:, 0:1],
                                                 ccol[:, 3:4], ALU.add),
             reads=[r_cc], writes=[r_cc])
        p.op("vector", lambda e: e.reciprocal(ccol[:, 5:6], ccol[:, 4:5]),
             reads=[r_cc], writes=[r_cc])
        p.op("vector", lambda e: e.tensor_scalar(pco[:], prb[:], ccol[:, 3:4], None,
                                                 ALU.is_ge),
             reads=[r_prb, r_cc], writes=[r_cb], label=f"cge{n}")
        p.op("vector", lambda e: e.tensor_tensor(pco[:], prb[:], pco[:], ALU.mult),
             reads=[r_prb, r_cb], writes=[r_cb])
        p.op("vector", lambda e: e.tensor_scalar(cbb[:], pco[:], ccol[:, 5:6], None,
                                                 ALU.mult),
             reads=[r_cb, r_cc], writes=[r_cb], label=f"ccomb{n}")
        p.op("tensor", lambda e: e.transpose(ps_ctr[:], cbb[:], identb[:]),
             reads=[r_cb, r_const], writes=[r_pctr], label=f"ctr{n}")
        p.op("vector", lambda e, n=n: e.tensor_copy(combf[0:8, n * 128:(n + 1) * 128],
                                                    ps_ctr[:]),
             reads=[r_pctr], writes=[r_combf], label=f"ctre{n}")
    scp.close()
    p.barrier()

    ps_g = [sc.ps(f"p7_psg{i}", [128, 512], F32) for i in range(2)]
    ps_u = [sc.ps(f"p7_psu{i}", [128, 512], F32) for i in range(2)]
    ps_d = [sc.ps(f"p7_psd{i}", [128, 512], F32) for i in range(2)]
    ps_sg = sc.ps("p7_pssg", [1, 512], F32)
    ps_rep = sc.ps("p7_psrep", [128, 512], F32)

    for tc in range(2):
        for c in range(NC16):
            p.dma("gpsimd", xba[:],
                  d["xn1"].ap()[c * 128:(c + 1) * 128, tc * 1024:(tc + 1) * 1024],
                  reads=[dr["xn1"]], writes=[r_xba], label=f"ldxna{tc}.{c}")
            p.dma("gpsimd", xbb[:],
                  d["xn1"].ap()[c * 128:(c + 1) * 128,
                                TH + tc * 1024:TH + (tc + 1) * 1024],
                  reads=[dr["xn1"]], writes=[r_xbb], label=f"ldxnb{tc}.{c}")
            p.op("vector", lambda e, c=c: e.tensor_scalar(
                sxn3[:, c, :], xbb[:], flagrep7[:, 0:1], None, ALU.mult),
                reads=[r_xbb, r_fl7], writes=[r_xn], label=f"xnd{tc}.{c}")
            p.op("vector", lambda e: e.tensor_scalar(
                xbb[:], xba[:], invflag7[:, 0:1], None, ALU.mult),
                reads=[r_xba, r_fl7], writes=[r_xbb], label=f"xnf{tc}.{c}")
            p.op("vector", lambda e, c=c: e.tensor_tensor(
                sxn3[:, c, :], sxn3[:, c, :], xbb[:], ALU.add),
                reads=[r_xbb, r_xn], writes=[r_xn], label=f"xns{tc}.{c}")
        for tsub in range(2):
            for c in range(NC16):
                p.op("tensor", lambda e, c=c, tsub=tsub: e.matmul(
                    ps_sg[:], shgb[:, c:c + 1], sxn3[:, c, tsub * 512:(tsub + 1) * 512],
                    start=(c == 0), stop=(c == NC16 - 1)),
                    reads=[r_const, r_xn], writes=[r_pssg], label=f"sgmm{tc}.{tsub}")
            p.op("scalar", lambda e, tsub=tsub: e.activation(
                srow[0:1, tsub * 512:(tsub + 1) * 512],
                ps_sg[0:1, :], AF.Sigmoid),
                reads=[r_pssg], writes=[r_srow], label=f"sgact{tc}.{tsub}")
            p.op("tensor", lambda e, tsub=tsub: e.matmul(
                ps_rep[:], onesf[:, :],
                srow[0:1, tsub * 512:(tsub + 1) * 512],
                start=True, stop=True),
                reads=[r_srow, r_const], writes=[r_psrep], label=f"sgrep{tc}.{tsub}")
            p.op("vector", lambda e, tsub=tsub: e.tensor_copy(
                sgr[:, tsub * 512:(tsub + 1) * 512], ps_rep[:]),
                reads=[r_psrep], writes=[r_sgr], label=f"sgre{tc}.{tsub}")

        for e_ in range(NE):
            is_sh = False
            nhc = NHE
            if True:
                for tsub in range(2):
                    p.op("tensor", lambda e, e_=e_, tc=tc, tsub=tsub: e.matmul(
                        ps_rep[:], eselb[:, e_ * 128:(e_ + 1) * 128],
                        combf[0:8, tc * 1024 + tsub * 512:tc * 1024 + (tsub + 1) * 512],
                        start=True, stop=True),
                        reads=[r_combf, r_const], writes=[r_psrep],
                        label=f"cre{tc}.{e_}.{tsub}")
                    p.op("vector", lambda e, tsub=tsub: e.tensor_copy(
                        combr[:, tsub * 512:(tsub + 1) * 512], ps_rep[:]),
                        reads=[r_psrep], writes=[r_combr], label=f"crev{tc}.{e_}.{tsub}")
            for hc in range(nhc):
                wb = hc % 2
                p.dma("gpsimd", wge3[wb][:, :, :],
                      d["wgate_e"].ap()[e_, :, :].rearrange("(c p) h -> p c h", p=128)
                      [:, :, hc * 128:(hc + 1) * 128],
                      writes=[r_wge[wb]], label=f"ldge{tc}.{e_}.{hc}")
                p.dma("gpsimd", wue3[wb][:, :, :],
                      d["wup_e"].ap()[e_, :, :].rearrange("(c p) h -> p c h", p=128)
                      [:, :, hc * 128:(hc + 1) * 128],
                      writes=[r_wue[wb]], label=f"ldue{tc}.{e_}.{hc}")
                for tsub in range(2):
                    pi = tsub
                    for c in range(NC16):
                        p.op("tensor", lambda e, wb=wb, c=c, tsub=tsub, pi=pi: e.matmul(
                            ps_g[pi][:], wge3[wb][:, c, :],
                            sxn3[:, c, tsub * 512:(tsub + 1) * 512],
                            start=(c == 0), stop=(c == NC16 - 1)),
                            reads=[r_wge[wb], r_xn], writes=[r_psg[pi]],
                            label=f"gmm{tc}.{e_}.{hc}.{tsub}")
                    for c in range(NC16):
                        p.op("tensor", lambda e, wb=wb, c=c, tsub=tsub, pi=pi: e.matmul(
                            ps_u[pi][:], wue3[wb][:, c, :],
                            sxn3[:, c, tsub * 512:(tsub + 1) * 512],
                            start=(c == 0), stop=(c == NC16 - 1)),
                            reads=[r_wue[wb], r_xn], writes=[r_psu[pi]],
                            label=f"umm{tc}.{e_}.{hc}.{tsub}")
                    p.op("scalar", lambda e, pi=pi: e.activation(
                        sgv[:], ps_g[pi][:], AF.Silu),
                        reads=[r_psg[pi]], writes=[r_sgv],
                        label=f"gsil{tc}.{e_}.{hc}.{tsub}")
                    p.op("vector", lambda e, pi=pi: e.tensor_copy(suv[:], ps_u[pi][:]),
                         reads=[r_psu[pi]], writes=[r_suv],
                         label=f"uev{tc}.{e_}.{hc}.{tsub}")
                    p.op("vector", lambda e: e.tensor_tensor(h1v[:], sgv[:], suv[:],
                                                             ALU.mult),
                         reads=[r_sgv, r_suv], writes=[r_h1v],
                         label=f"hm{tc}.{e_}.{hc}.{tsub}")
                    scl, rscl = (combr, r_combr)
                    p.op("gpsimd", lambda e, hc=hc, tsub=tsub, scl=scl: e.tensor_tensor(
                        hb3[:, hc, tsub * 512:(tsub + 1) * 512], h1v[:],
                        scl[:, tsub * 512:(tsub + 1) * 512], ALU.mult),
                        reads=[r_h1v, rscl], writes=[r_hb[hc]],
                        label=f"hs{tc}.{e_}.{hc}.{tsub}")
            for jg in range(4):
                p.dma("gpsimd", wde3[:, 0:NHE, :],
                      d["wdown_e"].ap()[e_, :, :].rearrange("(c p) j -> p c j", p=128)
                      [:, :, jg * 512:(jg + 1) * 512],
                      writes=[r_wde], label=f"ldde{tc}.{e_}.{jg}")
                for jj in range(4):
                    for tsub in range(2):
                        pi = tsub
                        jc = jg * 4 + jj
                        for c in range(nhc):
                            p.op("tensor", lambda e, c=c, jj=jj, tsub=tsub, pi=pi,
                                 nhc=nhc: e.matmul(
                                ps_d[pi][:], wde3[:, c, jj * 128:(jj + 1) * 128],
                                hb3[:, c, tsub * 512:(tsub + 1) * 512],
                                start=(c == 0), stop=(c == nhc - 1)),
                                reads=[r_wde, r_hb[c]], writes=[r_psd[pi]],
                                label=f"dmm{tc}.{e_}.{jc}.{tsub}")
                        ai = jc * 2 + tsub
                        if e_ == 0:
                            p.op("vector", lambda e, jc=jc, tsub=tsub, pi=pi: e.tensor_copy(
                                acc3[:, jc, tsub * 512:(tsub + 1) * 512], ps_d[pi][:]),
                                reads=[r_psd[pi]], writes=[r_acc[ai]],
                                label=f"dacc0{tc}.{jc}.{tsub}")
                        else:
                            p.op("vector", lambda e, jc=jc, tsub=tsub, pi=pi: e.tensor_tensor(
                                acc3[:, jc, tsub * 512:(tsub + 1) * 512],
                                acc3[:, jc, tsub * 512:(tsub + 1) * 512],
                                ps_d[pi][:], ALU.add),
                                reads=[r_psd[pi], r_acc[ai]], writes=[r_acc[ai]],
                                label=f"dacc{tc}.{e_}.{jc}.{tsub}")
        for tsub in range(2):
            pi = tsub
            for hc in range(NHS):
                wb = hc % 2
                p.dma("gpsimd", wge3[wb][:, :, :],
                      d["shwgate"].ap().rearrange("(c p) h -> p c h", p=128)
                      [:, :, hc * 128:(hc + 1) * 128],
                      writes=[r_wge[wb]], label=f"ldsg{tc}.{tsub}.{hc}")
                p.dma("gpsimd", wue3[wb][:, :, :],
                      d["shwup"].ap().rearrange("(c p) h -> p c h", p=128)
                      [:, :, hc * 128:(hc + 1) * 128],
                      writes=[r_wue[wb]], label=f"ldsu{tc}.{tsub}.{hc}")
                for c in range(NC16):
                    p.op("tensor", lambda e, wb=wb, c=c, tsub=tsub, pi=pi: e.matmul(
                        ps_g[pi][:], wge3[wb][:, c, :],
                        sxn3[:, c, tsub * 512:(tsub + 1) * 512],
                        start=(c == 0), stop=(c == NC16 - 1)),
                        reads=[r_wge[wb], r_xn], writes=[r_psg[pi]],
                        label=f"sgmm2{tc}.{tsub}.{hc}")
                for c in range(NC16):
                    p.op("tensor", lambda e, wb=wb, c=c, tsub=tsub, pi=pi: e.matmul(
                        ps_u[pi][:], wue3[wb][:, c, :],
                        sxn3[:, c, tsub * 512:(tsub + 1) * 512],
                        start=(c == 0), stop=(c == NC16 - 1)),
                        reads=[r_wue[wb], r_xn], writes=[r_psu[pi]],
                        label=f"summ{tc}.{tsub}.{hc}")
                p.op("scalar", lambda e, pi=pi: e.activation(
                    sgv[:], ps_g[pi][:], AF.Silu),
                    reads=[r_psg[pi]], writes=[r_sgv], label=f"ssil{tc}.{tsub}.{hc}")
                p.op("vector", lambda e, pi=pi: e.tensor_copy(suv[:], ps_u[pi][:]),
                     reads=[r_psu[pi]], writes=[r_suv], label=f"suev{tc}.{tsub}.{hc}")
                p.op("vector", lambda e: e.tensor_tensor(h1v[:], sgv[:], suv[:],
                                                         ALU.mult),
                     reads=[r_sgv, r_suv], writes=[r_h1v],
                     label=f"shm{tc}.{tsub}.{hc}")
                p.op("gpsimd", lambda e, hc=hc, tsub=tsub: e.tensor_tensor(
                    hb3s[:, hc, :], h1v[:],
                    sgr[:, tsub * 512:(tsub + 1) * 512], ALU.mult),
                    reads=[r_h1v, r_sgr], writes=[r_hb[hc]],
                    label=f"shs{tc}.{tsub}.{hc}")
            for jg in range(4):
                p.dma("gpsimd", wde3[:, 0:NHS, :],
                      d["shwdown"].ap().rearrange("(c p) j -> p c j", p=128)
                      [:, :, jg * 512:(jg + 1) * 512],
                      writes=[r_wde], label=f"ldsd{tc}.{tsub}.{jg}")
                for jj in range(4):
                    jc = jg * 4 + jj
                    for c in range(NHS):
                        p.op("tensor", lambda e, c=c, jj=jj, pi=pi: e.matmul(
                            ps_d[pi][:], wde3[:, c, jj * 128:(jj + 1) * 128],
                            hb3s[:, c, :],
                            start=(c == 0), stop=(c == NHS - 1)),
                            reads=[r_wde, r_hb[c]], writes=[r_psd[pi]],
                            label=f"sdmm{tc}.{tsub}.{jc}")
                    ai = jc * 2 + tsub
                    p.op("vector", lambda e, jc=jc, tsub=tsub, pi=pi: e.tensor_tensor(
                        acc3[:, jc, tsub * 512:(tsub + 1) * 512],
                        acc3[:, jc, tsub * 512:(tsub + 1) * 512],
                        ps_d[pi][:], ALU.add),
                        reads=[r_psd[pi], r_acc[ai]], writes=[r_acc[ai]],
                        label=f"sdacc{tc}.{tsub}.{jc}")

        for j in range(NC16):
            for tsub in range(2):
                oi = (j * 2 + tsub) % 2
                gt = tc * 1024 + tsub * 512
                p.dma("sync", att[:], a1r[:, j, gt:gt + 512],
                      reads=[dr["attn1"]], writes=[r_att], label=f"lda1{tc}.{j}.{tsub}")
                p.dma("sync", x1a[:], d["x1"].ap()[j * 128:(j + 1) * 128, gt:gt + 512],
                      reads=[dr["x1"]], writes=[r_x1a], label=f"ldx1a{tc}.{j}.{tsub}")
                p.dma("sync", x1b[:],
                      d["x1"].ap()[j * 128:(j + 1) * 128, TH + gt:TH + gt + 512],
                      reads=[dr["x1"]], writes=[r_x1b], label=f"ldx1b{tc}.{j}.{tsub}")
                p.op("vector", lambda e, j=j, tsub=tsub, oi=oi: e.tensor_tensor(
                    outt[oi][:], acc3[:, j, tsub * 512:(tsub + 1) * 512], att[:],
                    ALU.add),
                    reads=[r_acc[j * 2 + tsub], r_att], writes=[r_outt[oi]],
                    label=f"fin1{tc}.{j}.{tsub}")
                p.op("vector", lambda e: e.tensor_tensor(
                    att[:], x1b[:], x1a[:], ALU.subtract),
                    reads=[r_x1a, r_x1b, r_att], writes=[r_att],
                    label=f"x1d{tc}.{j}.{tsub}")
                p.op("vector", lambda e: e.tensor_scalar(
                    att[:], att[:], flagrep7[:, 0:1], None, ALU.mult),
                    reads=[r_att, r_fl7], writes=[r_att], label=f"x1f{tc}.{j}.{tsub}")
                p.op("vector", lambda e, oi=oi: e.tensor_tensor(
                    outt[oi][:], outt[oi][:], att[:], ALU.add),
                    reads=[r_att, r_outt[oi]], writes=[r_outt[oi]],
                    label=f"fin2{tc}.{j}.{tsub}")
                p.op("vector", lambda e, oi=oi: e.tensor_tensor(
                    outt[oi][:], outt[oi][:], x1a[:], ALU.add),
                    reads=[r_x1a, r_outt[oi]], writes=[r_outt[oi]],
                    label=f"fin3{tc}.{j}.{tsub}")
                p.dma("sync", outr[:, j, gt:gt + 512], outt[oi][:],
                      reads=[r_outt[oi]], writes=[dr["outh"]],
                      label=f"stout{tc}.{j}.{tsub}")


def build_p7():
    nc = bass.Bass("TRN2", target_bir_lowering=False)
    d = {}
    d["xn1"] = nc.dram_tensor("xn1", [D, T], BF16, kind="ExternalInput")
    d["logits1"] = nc.dram_tensor("logits1", [T, NE], F32, kind="ExternalInput")
    d["x1"] = nc.dram_tensor("x1", [D, T], F32, kind="ExternalInput")
    d["attn1"] = nc.dram_tensor("attn1", [D, TH], F32, kind="ExternalInput")
    d["wgate_e"] = nc.dram_tensor("wgate_e", [NE, D, HE], F32, kind="ExternalInput")
    d["wup_e"] = nc.dram_tensor("wup_e", [NE, D, HE], F32, kind="ExternalInput")
    d["wdown_e"] = nc.dram_tensor("wdown_e", [NE, HE, D], F32, kind="ExternalInput")
    d["shgate_col"] = nc.dram_tensor("shgate_col", [D, 1], F32, kind="ExternalInput")
    d["shwgate"] = nc.dram_tensor("shwgate", [D, HS], F32, kind="ExternalInput")
    d["shwup"] = nc.dram_tensor("shwup", [D, HS], F32, kind="ExternalInput")
    d["shwdown"] = nc.dram_tensor("shwdown", [HS, D], F32, kind="ExternalInput")
    d["esel_bf"] = nc.dram_tensor("esel_bf", [8, 8 * 128], BF16, kind="ExternalInput")
    d["ident_bf"] = nc.dram_tensor("ident_bf", [128, 128], BF16, kind="ExternalInput")
    d["ones_row_f"] = nc.dram_tensor("ones_row_f", [1, 128], F32, kind="ExternalInput")
    d["offv"] = nc.dram_tensor("offv", [1, 64], I32, kind="ExternalInput")
    d["flag"] = nc.dram_tensor("flag", [1, 1], F32, kind="ExternalInput")
    d["outh"] = nc.dram_tensor("outh", [D, TH], F32, kind="ExternalOutput")
    dr = {k: Res("dram_" + k) for k in d}
    p = Prog(nc)
    sc = Scope(nc)
    emit_p7(p, d, dr, sc)
    sc.close()
    p.finalize()
    return nc



def build_fused():
    nc = bass.Bass("TRN2", target_bir_lowering=False)
    d = {}
    EI = "ExternalInput"
    d["xT"] = nc.dram_tensor("xT", [D, T], F32, kind=EI)
    d["wqkvuT"] = nc.dram_tensor("wqkvuT", [D, 4 * D], F32, kind=EI)
    d["woutT"] = nc.dram_tensor("woutT", [D, D], F32, kind=EI)
    d["wguT"] = nc.dram_tensor("wguT", [D, 2 * HF0], F32, kind=EI)
    d["wprojT"] = nc.dram_tensor("wprojT", [HF0, D], F32, kind=EI)
    d["wkT"] = nc.dram_tensor("wkT", [D, D], F32, kind=EI)
    d["wvT"] = nc.dram_tensor("wvT", [D, D], F32, kind=EI)
    d["wqT"] = nc.dram_tensor("wqT", [D, D], F32, kind=EI)
    d["woT"] = nc.dram_tensor("woT", [D, D], F32, kind=EI)
    d["gatewT"] = nc.dram_tensor("gatewT", [D, NE], F32, kind=EI)
    d["wgate_e"] = nc.dram_tensor("wgate_e", [NE, D, HE], F32, kind=EI)
    d["wup_e"] = nc.dram_tensor("wup_e", [NE, D, HE], F32, kind=EI)
    d["wdown_e"] = nc.dram_tensor("wdown_e", [NE, HE, D], F32, kind=EI)
    d["shgate_col"] = nc.dram_tensor("shgate_col", [D, 1], F32, kind=EI)
    d["shwgate"] = nc.dram_tensor("shwgate", [D, HS], F32, kind=EI)
    d["shwup"] = nc.dram_tensor("shwup", [D, HS], F32, kind=EI)
    d["shwdown"] = nc.dram_tensor("shwdown", [HS, D], F32, kind=EI)
    d["memcat"] = nc.dram_tensor("memcat", [4, NH, HD, 129], F32, kind=EI)
    d["lmT"] = nc.dram_tensor("lmT", [4, HD, NH], F32, kind=EI)
    d["kcT"] = nc.dram_tensor("kcT", [NH, 4], F32, kind=EI)
    d["gate_row"] = nc.dram_tensor("gate_row", [1, NH], F32, kind=EI)
    d["slope_row"] = nc.dram_tensor("slope_row", [1, NH], F32, kind=EI)
    d["idxT"] = nc.dram_tensor("idxT", [BLK, BLK], F32, kind=EI)
    d["trilT"] = nc.dram_tensor("trilT", [BLK, BLK], F32, kind=EI)
    d["arrq"] = nc.dram_tensor("arrq", [128, BLK], F32, kind=EI)
    d["arrk"] = nc.dram_tensor("arrk", [128, BLK], F32, kind=EI)
    d["maskT1"] = nc.dram_tensor("maskT1", [CHK, CHK], F32, kind=EI)
    d["ident_bf"] = nc.dram_tensor("ident_bf", [128, 128], BF16, kind=EI)
    d["ident_f"] = nc.dram_tensor("ident_f", [128, 128], F32, kind=EI)
    d["ones_col_bf"] = nc.dram_tensor("ones_col_bf", [128, 1], BF16, kind=EI)
    d["ones_col_f"] = nc.dram_tensor("ones_col_f", [128, 1], F32, kind=EI)
    d["ones_row_f"] = nc.dram_tensor("ones_row_f", [1, 128], F32, kind=EI)
    d["gsel_bf"] = nc.dram_tensor("gsel_bf", [128, 128], BF16, kind=EI)
    d["esel_bf"] = nc.dram_tensor("esel_bf", [8, 8 * 128], BF16, kind=EI)
    d["offv"] = nc.dram_tensor("offv", [1, 64], I32, kind=EI)
    d["flag"] = nc.dram_tensor("flag", [1, 1], F32, kind=EI)
    d["xn0"] = nc.dram_tensor("xn0", [D, T], BF16, kind="Internal")
    d["sq0"] = nc.dram_tensor("sq0", [D, T], BF16, kind="Internal")
    d["sk0"] = nc.dram_tensor("sk0", [D, T], BF16, kind="Internal")
    d["v0"] = nc.dram_tensor("v0", [T, D], BF16, kind="Internal")
    d["u0"] = nc.dram_tensor("u0", [T, D], BF16, kind="Internal")
    d["oraw0"] = nc.dram_tensor("oraw0", [T, D], BF16, kind="Internal")
    d["xa"] = nc.dram_tensor("xa", [D, T], F32, kind="Internal")
    d["h0"] = nc.dram_tensor("h0", [HF0, T], BF16, kind="Internal")
    d["hrstd"] = nc.dram_tensor("hrstd", [128, T], F32, kind="Internal")
    d["x1"] = nc.dram_tensor("x1", [D, T], F32, kind="Internal")
    d["xn1"] = nc.dram_tensor("xn1", [D, T], BF16, kind="Internal")
    d["sk1"] = nc.dram_tensor("sk1", [D, T], BF16, kind="Internal")
    d["v1"] = nc.dram_tensor("v1", [T, D], BF16, kind="Internal")
    d["logits1"] = nc.dram_tensor("logits1", [T, NE], F32, kind="Internal")
    d["sumxn"] = nc.dram_tensor("sumxn", [128, NC16], F32, kind="Internal")
    d["attn1"] = nc.dram_tensor("attn1", [D, TH], F32, kind="Internal")
    d["outh"] = nc.dram_tensor("outh", [D, TH], F32, kind="ExternalOutput")

    d["wguq"] = nc.dram_tensor("wguq", [D, 2 * HF0], BF16, kind="Internal")
    d["wprojq"] = nc.dram_tensor("wprojq", [HF0, D], BF16, kind="Internal")
    dr = {k: Res("dram_" + k) for k in d}
    p = Prog(nc)
    sc0 = Scope(nc)
    emit_p0quant(p, d, dr, sc0)
    sc = Scope(nc)
    emit_p1(p, d, sc)
    sc.close()
    p.barrier()
    sc0.close()
    for emitfn in (emit_p2, emit_p3):
        sc = Scope(nc)
        emitfn(p, d, sc)
        sc.close()
        p.barrier()
    sc = Scope(nc)
    emit_p4a(p, d, dr, sc)
    sc.close()
    p.barrier()
    sc = Scope(nc)
    emit_p4b(p, d, dr, sc)
    sc.close()
    p.barrier()
    for emitfn in (emit_p5, emit_p6, emit_p7):
        sc = Scope(nc)
        emitfn(p, d, dr, sc)
        sc.close()
        p.barrier()
    p.finalize()
    return nc


ALL_INPUT_NAMES = [
    "xT", "wqkvuT", "woutT", "wguT", "wprojT", "wkT", "wvT", "wqT", "woT",
    "gatewT", "wgate_e", "wup_e", "wdown_e", "shgate_col", "shwgate", "shwup",
    "shwdown", "memcat", "lmT", "kcT", "gate_row", "slope_row", "idxT", "trilT",
    "arrq", "arrk", "maskT1", "ident_bf", "ident_f", "ones_col_bf", "ones_col_f",
    "ones_row_f", "gsel_bf", "esel_bf", "offv", "flag",
]

FUSED = True


def kernel(**inputs):
    shared, per_core = _prep_host(inputs)
    if FUSED:
        nc = build_fused()
        in_maps = []
        for c in range(NCORES):
            m = {}
            for n in ALL_INPUT_NAMES:
                m[n] = shared[n] if n in shared else per_core[c][n]
            in_maps.append(m)
        res = run_bass_kernel_spmd(nc, in_maps, core_ids=list(range(NCORES)))
        out = np.empty((4, T, D), np.float32)
        for c in range(NCORES):
            b, half = c // 2, c & 1
            out[b, half * TH:(half + 1) * TH, :] = np.asarray(res.results[c]["outh"]).T
        return out
    return _kernel_unfused(shared, per_core)


def _kernel_unfused(shared, per_core):


    def run(nc, names, store):
        in_maps = []
        for c in range(NCORES):
            m = {}
            for n in names:
                if n in shared:
                    m[n] = shared[n]
                elif n in per_core[c]:
                    m[n] = per_core[c][n]
                else:
                    m[n] = store[c][n]
            in_maps.append(m)
        res = run_bass_kernel_spmd(nc, in_maps, core_ids=list(range(NCORES)))
        for c in range(NCORES):
            store[c].update(res.results[c])

    store = [{} for _ in range(NCORES)]
    run(build_p1(None), ["xT", "wqkvuT", "ones_col_bf", "ones_row_f"], store)
    run(build_p2(), ["sq0", "sk0", "v0", "slope_row", "idxT", "trilT", "arrq",
                     "arrk", "ident_bf", "ones_row_f"], store)
    run(build_p3(), ["oraw0", "u0", "woutT", "xT", "ident_bf"], store)
    for c in range(NCORES):
        for k in ("sq0", "sk0", "v0", "oraw0", "u0"):
            store[c].pop(k, None)
    run(build_p4(), ["xn0", "wguT", "wprojT", "xa", "ones_row_f", "ones_col_bf",
                     "gsel_bf"], store)
    for c in range(NCORES):
        for k in ("xn0", "xa"):
            store[c].pop(k, None)
    run(build_p5(), ["x1", "wkT", "wvT", "gatewT", "ones_col_bf", "ones_row_f"],
        store)
    run(build_p6(), ["xn1", "sk1", "v1", "sumxn", "wqT", "woT", "memcat", "lmT",
                     "kcT", "gate_row", "maskT1", "ident_bf", "ident_f",
                     "ones_row_f", "ones_col_bf", "ones_col_f", "offv", "flag"],
        store)
    for c in range(NCORES):
        for k in ("sk1", "v1", "sumxn"):
            store[c].pop(k, None)
    run(build_p7(), ["xn1", "logits1", "x1", "attn1", "wgate_e", "wup_e",
                     "wdown_e", "shgate_col", "shwgate", "shwup", "shwdown",
                     "esel_bf", "ident_bf", "ones_row_f", "offv", "flag"], store)

    out = np.empty((4, T, D), np.float32)
    for c in range(NCORES):
        b, half = c // 2, c & 1
        out[b, half * TH:(half + 1) * TH, :] = np.asarray(store[c]["outh"]).T
    return out
